# version 09
# baseline (speedup 1.0000x reference)
import numpy as np
from contextlib import ExitStack
import concourse.bass as bass
import concourse.mybir as mybir
from concourse.bass_utils import run_bass_kernel_spmd

F32 = mybir.dt.float32
BF16 = mybir.dt.bfloat16
AF = mybir.ActivationFunctionType
ALU = mybir.AluOpType
AX = mybir.AxisListType

B, T, D = 8, 2048, 2048
H, HKV, HD = 16, 4, 128
KV = HKV * HD
RANK = 16
DE = 8192
NT = T // 128
ND = D // 128
EPS_LN = 1e-5
EPS_RMS = float(np.finfo(np.float32).eps)
SCALE_QK = float(HD ** -0.5)

DEV = False
SKIP_LORA_B = False
TRUNC = 99


class Ctx:
    ENGS = ("sync", "gpsimd", "tensor", "vector", "scalar")

    def __init__(self, nc):
        self.nc = nc
        self.ops = {e: [] for e in self.ENGS}
        self.cnt = {}
        self.h = {}

    def new_sem(self, name, stack):
        self.h[name] = stack.enter_context(self.nc.semaphore(name))
        self.cnt[name] = 0

    def wait(self, eng, sem, tgt=None):
        if tgt is None:
            tgt = self.cnt[sem]
        assert tgt <= self.cnt[sem], f"wait on future inc: {sem} {tgt} > {self.cnt[sem]}"
        h = self.h[sem]
        self.ops[eng].append(lambda e, h=h, tgt=tgt: e.wait_ge(h, tgt))

    def emit(self, eng, fn, inc=None, inc_amt=1):
        if inc is not None:
            hh = self.h[inc]
            self.cnt[inc] += inc_amt
            self.ops[eng].append(
                lambda e, fn=fn, hh=hh, a=inc_amt: fn(e).then_inc(hh, a))
        else:
            self.ops[eng].append(fn)
        return self.cnt.get(inc)

    def dma(self, eng, out, in_, sem=None):
        if sem is None:
            sem = "w" if eng == "sync" else "g"
        return self.emit(eng, lambda e, o=out, i=in_: e.dma_start(out=o, in_=i),
                         inc=sem, inc_amt=16)

    def phase_barrier(self):
        for eng in self.ENGS:
            for sem in self.cnt:
                if self.cnt[sem] > 0:
                    self.wait(eng, sem, self.cnt[sem])

    def finalize(self, block):
        for eng in self.ENGS:
            def run(e, ops=self.ops[eng]):
                for f in ops:
                    f(e)
            getattr(block, eng)(run)


def make_consts():
    inv_freq = (1.0 / (np.float32(10000.0) **
                       (np.arange(0, 64, 2, dtype=np.float32) / np.float32(64.0))))
    freqs = np.arange(T, dtype=np.float32)[:, None] * inv_freq[None, :]
    cosT = np.cos(freqs).astype(np.float32).T.copy()
    sinT = np.sin(freqs).astype(np.float32).T.copy()
    c2 = np.concatenate([cosT, cosT], axis=0)
    s2 = np.concatenate([sinT, -sinT], axis=0)
    ident = np.eye(128, dtype=np.float32)
    perm = np.zeros((128, 128), dtype=np.float32)
    perm[np.arange(32), np.arange(32) + 32] = 1.0
    perm[np.arange(32) + 32, np.arange(32)] = 1.0
    for i in range(64, 128):
        perm[i, i] = 1.0
    tri = np.where(np.arange(128)[None, :] <= np.arange(128)[:, None],
                   np.float32(0.0), np.float32(-1e30))
    maskd = np.zeros((128, 4 * 512), dtype=np.float32)
    for k in range(4):
        maskd[:, k * 512 + k * 128: k * 512 + (k + 1) * 128] = tri
    ones_j = np.ones((128, 128), dtype=np.float32)
    ones_row = np.ones((1, 128), dtype=np.float32)
    return dict(c2=c2, s2=s2, perm=perm, ident=ident, maskd=maskd,
                ones_j=ones_j, ones_row=ones_row)


def build_program(dev_stop=None, dev_dumps=(), p1_on=True):
    nc = bass.Bass("TRN2", target_bir_lowering=False)
    C = make_consts()

    xb = nc.dram_tensor("xb", [T, D], F32, kind="ExternalInput").ap()
    w_q = nc.dram_tensor("Wq", [D, D], F32, kind="ExternalInput").ap()
    w_k = nc.dram_tensor("Wk", [D, KV], F32, kind="ExternalInput").ap()
    w_v = nc.dram_tensor("Wv", [D, KV], F32, kind="ExternalInput").ap()
    w_p = nc.dram_tensor("Wproj", [D, D], F32, kind="ExternalInput").ap()
    gain = nc.dram_tensor("q_gain", [1, H], F32, kind="ExternalInput").ap()
    a_q = nc.dram_tensor("q_A", [D, RANK], F32, kind="ExternalInput").ap()
    a_k = nc.dram_tensor("k_A", [D, RANK], F32, kind="ExternalInput").ap()
    a_v = nc.dram_tensor("v_A", [D, RANK], F32, kind="ExternalInput").ap()
    a_p = nc.dram_tensor("p_A", [D, RANK], F32, kind="ExternalInput").ap()
    b_q = nc.dram_tensor("q_B", [RANK, D], F32, kind="ExternalInput").ap()
    b_k = nc.dram_tensor("k_B", [RANK, KV], F32, kind="ExternalInput").ap()
    b_v = nc.dram_tensor("v_B", [RANK, KV], F32, kind="ExternalInput").ap()
    b_p = nc.dram_tensor("p_B", [RANK, D], F32, kind="ExternalInput").ap()
    w_up = nc.dram_tensor("W_up", [DE, D], F32, kind="ExternalInput").ap()
    w_gate = nc.dram_tensor("W_gate", [DE, D], F32, kind="ExternalInput").ap()
    w_down = nc.dram_tensor("W_down", [D, DE], F32, kind="ExternalInput").ap()

    out = nc.dram_tensor("out", [T, D], F32, kind="ExternalOutput").ap()

    y_dram = nc.dram_tensor("y_scr", [D, T], BF16, kind="Internal").ap()
    xmid_dram = nc.dram_tensor("xmid_scr", [T, D], F32, kind="Internal").ap()
    hid_dram = nc.dram_tensor("hid_scr", [DE, T], BF16, kind="Internal").ap()
    tdT_dram = nc.dram_tensor("tdT_scr", [DE, D], BF16, kind="Internal").ap()

    c2_d = nc.inline_tensor(C["c2"], "c2c").ap()
    s2_d = nc.inline_tensor(C["s2"], "s2c").ap()
    ident_d = nc.inline_tensor(C["ident"], "identc").ap()
    maskd_d = nc.inline_tensor(C["maskd"], "maskd").ap()
    onesj_d = nc.inline_tensor(C["ones_j"], "onesj").ap()
    onesr_d = nc.inline_tensor(C["ones_row"], "onesr").ap()

    dev = {}
    if DEV:
        for name, shape, dt in dev_dumps:
            dev[name] = nc.dram_tensor(name, shape, dt, kind="ExternalOutput").ap()

    with ExitStack() as outer:
        ctx = Ctx(nc)
        for s in ("w", "g", "t", "v", "s", "wa", "wb", "wc", "ws",
                  "ga", "gb", "gy", "gy2", "wy", "wy2", "wx", "wx2", "gm", "gh",
                  "gt2", "go", "go2", "wh", "u0", "u1", "u2", "u3"):
            ctx.new_sem(s, outer)

        slabA_h = outer.enter_context(nc.sbuf_tensor("slabA", [128, ND * T], BF16))
        slabA = slabA_h[:].rearrange("p (dt t) -> p dt t", dt=ND)
        w1 = outer.enter_context(nc.sbuf_tensor("w1", [128, 2048], F32))
        w2 = outer.enter_context(nc.sbuf_tensor("w2", [128, 2048], F32))
        b1 = outer.enter_context(nc.sbuf_tensor("b1", [128, 2048], BF16))
        b2 = outer.enter_context(nc.sbuf_tensor("b2", [128, 2048], BF16))
        st = outer.enter_context(nc.sbuf_tensor("stt", [128, 104], F32))
        id_bf = outer.enter_context(nc.sbuf_tensor("id_bf", [128, 128], BF16))
        id_f = outer.enter_context(nc.sbuf_tensor("id_f", [128, 128], F32))
        prow = outer.enter_context(nc.sbuf_tensor("prow", [1, 128], F32))
        zrow = outer.enter_context(nc.sbuf_tensor("zrow", [1, 128], F32))
        srow = outer.enter_context(nc.sbuf_tensor("srow", [1, 128], F32))
        ptmp = outer.enter_context(nc.sbuf_tensor("ptmp", [128, 32], F32))
        mask_bf = outer.enter_context(nc.sbuf_tensor("mask_bf", [128, 2048], BF16))
        jj_bf = outer.enter_context(nc.sbuf_tensor("jj_bf", [128, 128], BF16))
        onesr_f = outer.enter_context(nc.sbuf_tensor("onesr_f", [1, 128], F32))
        epsc = outer.enter_context(nc.sbuf_tensor("epsc", [128, 4], F32))
        pacc = outer.enter_context(nc.sbuf_tensor("pacc", [128, 768], F32))
        jj_f = outer.enter_context(nc.sbuf_tensor("jj_f", [128, 128], F32))
        scl = outer.enter_context(nc.sbuf_tensor("scl", [128, 64], F32))
        devcap = outer.enter_context(nc.sbuf_tensor("devcap", [128, 104], F32)) if DEV else None

        ctx.dma("gpsimd", id_bf[:], ident_d)
        ctx.dma("gpsimd", id_f[:], ident_d)
        ctx.dma("gpsimd", mask_bf[:], maskd_d)
        ctx.dma("gpsimd", jj_bf[:], onesj_d)
        ctx.dma("gpsimd", jj_f[:], onesj_d)
        ctx.dma("gpsimd", onesr_f[:], onesr_d)
        g_consts = ctx.cnt["g"]
        ctx.emit("vector", lambda e: e.memset(st[:], 0.0))
        ctx.emit("vector", lambda e: e.memset(epsc[:, 0:1], EPS_LN))
        ctx.emit("vector", lambda e: e.memset(epsc[:, 1:2], EPS_RMS))
        ctx.emit("vector", lambda e: e.memset(epsc[:, 2:3], 1e-24))
        ctx.emit("vector", lambda e: e.memset(zrow[:], 0.0))
        ctx.emit("vector", lambda e: e.memset(scl[:], 0.0), inc="v")
        ctx.wait("vector", "v")

        ctx.emit("vector", lambda e: e.memset(scl[:, 48:49], 1048576.0), inc="v")
        ctx.wait("vector", "v")
        for it in range(40):
            esrc = 48 if it % 2 == 0 else 52
            edst = 52 if it % 2 == 0 else 48
            ctx.emit("vector", lambda e, esrc=esrc: e.tensor_scalar(
                scl[:, 49:50], scl[:, esrc:esrc + 1], 1.0, None, ALU.add),
                inc="v")
            ctx.wait("vector", "v")
            ctx.emit("vector", lambda e, esrc=esrc: e.tensor_tensor(
                scl[:, 50:51], scl[:, 49:50], scl[:, esrc:esrc + 1],
                ALU.is_equal), inc="v")
            ctx.wait("vector", "v")
            ctx.emit("vector", lambda e: e.tensor_scalar(
                scl[:, 51:52], scl[:, 50:51], -1.0, 2.0, ALU.mult, ALU.add),
                inc="v")
            ctx.wait("vector", "v")
            ctx.emit("vector", lambda e, esrc=esrc, edst=edst: e.tensor_tensor(
                scl[:, edst:edst + 1], scl[:, esrc:esrc + 1], scl[:, 51:52],
                ALU.mult), inc="v")
            ctx.wait("vector", "v")
        ctx.emit("vector", lambda e: e.tensor_scalar(
            scl[:, 60:61], scl[:, 48:49], 2.0, None, ALU.mult), inc="v")
        ctx.wait("vector", "v")

        with ExitStack() as p1s:
            xt0 = p1s.enter_context(nc.sbuf_tensor("xt0", [128, 2048], F32))
            xt1 = p1s.enter_context(nc.sbuf_tensor("xt1", [128, 2048], F32))
            ptr = p1s.enter_context(nc.psum_tensor("ptr_ln", [128, 2048], BF16))
            xts = [xt0, xt1]
            dma_done = [0] * NT
            norm_done = [0] * NT
            tpose_done = [0] * NT
            evac_done = [0] * NT
            for tt in range(NT):
                xt = xts[tt % 2]
                wsem = "wa" if tt % 2 == 0 else "wb"
                if tt >= 2:
                    ctx.wait("sync", "s", norm_done[tt - 2])
                dma_done[tt] = ctx.dma("sync", xt[:], xb[tt * 128:(tt + 1) * 128, :],
                                       sem=wsem)
                ctx.wait("vector", wsem)
                ctx.emit("vector", lambda e, xt=xt: e.reduce_sum(
                    st[:, 0:1], xt[:], axis=AX.X), inc="v")
                ctx.wait("vector", "v")
                ctx.wait("scalar", wsem)
                if tt == 0:
                    ctx.wait("scalar", "v", ctx.cnt["v"])
                ctx.emit("scalar", lambda e, xt=xt: e.activation(
                    b2[:], xt[:], AF.Square, accum_out=st[:, 8:9]), inc="s")
                ctx.wait("vector", "s", ctx.cnt["s"])
                ctx.emit("vector", lambda e: e.tensor_scalar(
                    st[:, 16:17], st[:, 0:1], 1.0 / D, None, ALU.mult))
                ctx.emit("vector", lambda e: e.tensor_scalar(
                    st[:, 24:25], st[:, 8:9], 1.0 / D, None, ALU.mult), inc="v")
                ctx.wait("vector", "v")
                ctx.emit("vector", lambda e: e.scalar_tensor_tensor(
                    st[:, 32:33], st[:, 16:17], -1.0, st[:, 16:17], ALU.mult, ALU.mult),
                    inc="v")
                ctx.wait("vector", "v")
                ctx.emit("vector", lambda e: e.tensor_tensor(
                    st[:, 40:41], st[:, 24:25], st[:, 32:33], ALU.add), inc="v")
                ctx.wait("scalar", "v", ctx.cnt["v"])
                ctx.emit("scalar", lambda e: e.activation(
                    st[:, 48:49], st[:, 40:41], AF.Ln, bias=epsc[:, 0:1], scale=1.0),
                    inc="s")
                ctx.wait("scalar", "s")
                ctx.emit("scalar", lambda e: e.activation(
                    st[:, 56:57], st[:, 48:49], AF.Exp, bias=0.0, scale=-0.5), inc="s")
                ctx.wait("vector", "s", ctx.cnt["s"])
                ctx.emit("vector", lambda e: e.scalar_tensor_tensor(
                    st[:, 64:65], st[:, 16:17], -1.0, st[:, 56:57], ALU.mult, ALU.mult),
                    inc="v")
                if DEV and tt == 0:
                    ctx.wait("vector", "s", ctx.cnt["s"])
                    ctx.wait("vector", "v")
                    ctx.emit("vector", lambda e: e.tensor_copy(devcap[:], st[:]), inc="v")
                ctx.wait("scalar", "v", ctx.cnt["v"])
                if tt >= 1:
                    ctx.wait("scalar", "t", tpose_done[tt - 1])
                norm_done[tt] = ctx.emit("scalar", lambda e, xt=xt: e.activation(
                    b1[:], xt[:], AF.Identity, bias=st[:, 64:65], scale=st[:, 56:57]),
                    inc="s")
                if tt == 0:
                    ctx.wait("tensor", "g", g_consts)
                ctx.wait("tensor", "s", norm_done[tt])
                if tt >= 1:
                    ctx.wait("tensor", "v", evac_done[tt - 1])
                for grp in range(4):
                    half = (grp % 2) * 1024
                    if grp >= 2:
                        ctx.wait("tensor", "v", ctx.cnt["v"])
                    for j in range(4):
                        dt = grp * 4 + j
                        ctx.emit("tensor", lambda e, dt=dt, half=half, j=j:
                                 e.transpose(ptr[:, half + j * 128: half + (j + 1) * 128],
                                             b1[:, dt * 128:(dt + 1) * 128], id_bf[:]),
                                 inc=("t" if True else None))
                    ctx.wait("vector", "t", ctx.cnt["t"])
                    ctx.emit("vector", lambda e, grp=grp, half=half, tt=tt:
                             e.tensor_copy(
                                 slabA[:, grp * 4:(grp + 1) * 4, tt * 128:(tt + 1) * 128],
                                 ptr[:, half:half + 512].rearrange("p (j c) -> p j c", j=4)),
                             inc="v")
                tpose_done[tt] = ctx.cnt["t"]
                evac_done[tt] = ctx.cnt["v"]

        if DEV and "dev_xn" in dev:
            ctx.wait("gpsimd", "v", ctx.cnt["v"])
            ctx.wait("gpsimd", "s", ctx.cnt["s"])
            ctx.dma("gpsimd", dev["dev_xn"], slabA_h[:])
        if DEV and "dev_b1" in dev:
            ctx.wait("gpsimd", "s", ctx.cnt["s"])
            ctx.dma("gpsimd", dev["dev_b1"], b1[:])
        if DEV and "dev_st" in dev:
            ctx.wait("gpsimd", "s", ctx.cnt["s"])
            ctx.dma("gpsimd", dev["dev_st"], devcap[:])

        if dev_stop == "ln1":
            _final_waits(ctx)
            with nc.Block() as block:
                ctx.finalize(block)
            return nc

        ln1_done_v = ctx.cnt["v"]

        ctx.phase_barrier()
        with ExitStack() as p2s:
            sb = lambda n, s, d: p2s.enter_context(nc.sbuf_tensor(n, s, d))
            c2_s = sb("c2_s", [64, 2048], F32)
            s2_s = sb("s2_s", [64, 2048], F32)
            rb = sb("rb", [64, 2048], F32)
            rb2 = sb("rb2", [64, 2048], F32)
            wcol_bf = sb("wcol_bf", [128, 2048], BF16)
            k_bf = sb("k_bf", [128, 2048], BF16)
            q_bf = sb("q_bf", [128, 2048], BF16)
            vhat = sb("vhat", [128, 2048], F32)
            v_tok = sb("v_tok", [128, 2048], BF16)
            ybuf = sb("ybuf", [128, 1024], BF16)
            pt_bf = sb("pt_bf", [128, 16 * 512], BF16)
            rstdb = sb("rstdb", [128, 2048], F32)
            aq_s = sb("aq_s", [128, 256], BF16)
            ak_s = sb("ak_s", [128, 256], BF16)
            av_s = sb("av_s", [128, 256], BF16)
            ap_s = sb("ap_s", [128, 256], BF16)
            bmats_h = sb("bmats", [80, 2048], BF16)
            bq_s = bmats_h[0:16, :]
            bk_s = bmats_h[32:48, 0:512]
            bv_s = bmats_h[64:80, 0:512]
            lora1_h = sb("lora1", [80, 2048], BF16)
            qa_b = lora1_h[0:16, :]
            ka_b = lora1_h[32:48, :]
            va_b = lora1_h[64:80, :]
            gain_s = sb("gain_s", [1, 16], F32)
            gams = sb("gams", [128, 16], F32)
            pdump = sb("pdump", [128, 1024], F32)
            pdump1 = sb("pdump1", [128, 1024], F32)
            pw0 = sb("pw0", [128, 1024], F32)
            pw1 = sb("pw1", [128, 1024], F32)
            a_ps = p2s.enter_context(nc.psum_tensor("a_ps", [128, 2048], F32))
            pt_ps = p2s.enter_context(nc.psum_tensor("pt_ps", [128, 2048], BF16))
            y_ps = p2s.enter_context(nc.psum_tensor("y_ps", [128, 512], F32))
            cb_ps = p2s.enter_context(nc.psum_tensor("cb_ps", [128, 512], F32))

            for j in range(16):
                sc = (j % 4) * 128
                if sc:
                    ctx.emit("vector", lambda e, j=j, sc=sc: e.memset(
                        pt_bf[:, j * 512: j * 512 + sc], 0.0), inc="v")
            ctx.wait("vector", "v")

            ctx.dma("gpsimd", c2_s[:], c2_d)
            ctx.dma("gpsimd", s2_s[:], s2_d)
            for dst, src in ((aq_s, a_q), (ak_s, a_k), (av_s, a_v), (ap_s, a_p)):
                ctx.dma("gpsimd", dst[:].rearrange("p (dt r) -> p dt r", dt=ND),
                        src.rearrange("(dt p) r -> p dt r", p=128))
            ctx.dma("gpsimd", bq_s, b_q)
            ctx.dma("gpsimd", bk_s, b_k)
            ctx.dma("gpsimd", bv_s, b_v)
            ctx.dma("gpsimd", gain_s[:], gain)
            g_setup = ctx.cnt["g"]
            ctx.wait("tensor", "g", g_setup)
            ctx.emit("tensor", lambda e: e.matmul(
                cb_ps[:, 0:16], onesr_f[:], gain_s[:], start=True, stop=True),
                inc="t")
            ctx.wait("vector", "t", ctx.cnt["t"])
            ctx.emit("vector", lambda e: e.tensor_scalar(
                gams[:], cb_ps[:, 0:16], SCALE_QK, None, ALU.mult), inc="v")
            gams_done = ctx.cnt["v"]

            pass1_srcs = []
            for mi, wmat in enumerate((w_down, w_up, w_gate)):
                flat = wmat.rearrange("r c -> (r c)").rearrange(
                    "(b c x) -> c b x", b=128, x=1024)
                for ch in range(128):
                    pass1_srcs.append((mi, ch, flat[ch]))
            n_p1 = len(pass1_srcs)
            assert n_p1 == 384, n_p1
            p1_dma = [0] * n_p1
            p1_use = [0] * n_p1

            def emit_pass1(i):
                mi, ch, src = pass1_srcs[i]
                buf = pw0 if i % 2 == 0 else pw1
                dmp = pdump if i % 2 == 0 else pdump1
                psem = "ga" if i % 2 == 0 else "gb"
                if i >= 2:
                    ctx.wait("gpsimd", "s", p1_use[i - 2])
                p1_dma[i] = ctx.dma("gpsimd", buf[:], src, sem=psem)
                ctx.wait("scalar", psem)
                p1_use[i] = ctx.emit("scalar", lambda e, buf=buf, dmp=dmp, i=i:
                                     e.activation(
                                         dmp[:], buf[:], AF.Abs,
                                         accum_out=pacc[:, i:i + 1]), inc="s")

            ctx.wait("tensor", "v", gams_done)
            ctx.wait("tensor", "v", ln1_done_v)
            for amat, outb in ((aq_s, qa_b), (ak_s, ka_b), (av_s, va_b)):
                for tc in range(4):
                    ctx.wait("tensor", "s", ctx.cnt["s"])
                    for dt in range(ND):
                        ctx.emit("tensor", lambda e, amat=amat, dt=dt, tc=tc:
                                 e.matmul(cb_ps[0:16, 0:512],
                                          amat[:, dt * 16:(dt + 1) * 16],
                                          slabA[:, dt, tc * 512:(tc + 1) * 512],
                                          start=(dt == 0), stop=(dt == ND - 1)),
                                 inc=("t" if dt == ND - 1 else None))
                    ctx.wait("scalar", "t", ctx.cnt["t"])
                    ctx.emit("scalar", lambda e, outb=outb, tc=tc: e.copy(
                        outb[0:16, tc * 512:(tc + 1) * 512], cb_ps[0:16, 0:512]),
                        inc="s")
            lora1_done_s = ctx.cnt["s"]
            if DEV and dev_stop == "k0":
                if "dev_ka_early" in dev:
                    ctx.wait("gpsimd", "s", ctx.cnt["s"])
                    ctx.dma("gpsimd", dev["dev_ka_early"], ka_b)
            if DEV and dev_stop == "lora0":
                ctx.wait("gpsimd", "s", ctx.cnt["s"])
                ctx.wait("gpsimd", "v", ctx.cnt["v"])
                for nm, src in (("dev_qa", qa_b), ("dev_ka", ka_b),
                                ("dev_va", va_b), ("dev_xn", slabA_h[:])):
                    if nm in dev:
                        ctx.dma("gpsimd", dev[nm], src)
                _final_waits(ctx)
                with nc.Block() as block:
                    ctx.finalize(block)
                return nc

            p1_next = 0
            if not p1_on:
                def emit_pass1(i):
                    pass
            while p1_next < 4:
                emit_pass1(p1_next); p1_next += 1

            def build_row(wmat, col0, bmat, bcol0, stage1, is_q, hq, dst_bf,
                          want_vhat=False):
                ctx.wait("gpsimd", "t", wcol_guard[0])
                for dt in range(ND):
                    ctx.dma("gpsimd", wcol_bf[:, dt * 128:(dt + 1) * 128],
                            wmat[dt * 128:(dt + 1) * 128, col0:col0 + 128],
                            sem="wc")
                ctx.wait("tensor", "wc")
                ctx.wait("tensor", "s", ctx.cnt["s"])
                ctx.wait("tensor", "v", ctx.cnt["v"])
                for pair in range(2):
                    if pair == 1:
                        ctx.wait("tensor", "s", ctx.cnt["s"])
                    for dt in range(ND):
                        for sub in range(2):
                            tc = pair * 2 + sub
                            ctx.emit("tensor",
                                     lambda e, dt=dt, sub=sub, tc=tc: e.matmul(
                                         a_ps[:, sub * 512:(sub + 1) * 512],
                                         wcol_bf[:, dt * 128:(dt + 1) * 128],
                                         slabA[:, dt, tc * 512:(tc + 1) * 512],
                                         start=(dt == 0), stop=False))
                    for sub in range(2):
                        tc = pair * 2 + sub
                        if SKIP_LORA_B:
                            ctx.emit("tensor", lambda e, sub=sub: e.matmul(
                                a_ps[:, sub * 512:(sub + 1) * 512],
                                wcol_bf[:, 0:128],
                                slabA[:, 0, 0:512],
                                start=False, stop=True),
                                inc=("t" if sub == 1 else None))
                            continue
                        ctx.emit("tensor", lambda e, sub=sub, tc=tc, bmat=bmat,
                                 bcol0=bcol0, stage1=stage1: e.matmul(
                                     a_ps[:, sub * 512:(sub + 1) * 512],
                                     bmat[0:16, bcol0:bcol0 + 128],
                                     stage1[0:16, tc * 512:(tc + 1) * 512],
                                     start=False, stop=True),
                                 inc=("t" if sub == 1 else None))
                    ctx.wait("scalar", "t", ctx.cnt["t"])
                    ctx.emit("scalar", lambda e, pair=pair: e.copy(
                        w1[:, pair * 1024:(pair + 1) * 1024],
                        a_ps[:, 0:1024]), inc="s")
                evac_done = ctx.cnt["s"]
                wcol_guard[0] = ctx.cnt["t"]
                if TRUNC <= 1:
                    return ctx.cnt["v"]
                ctx.wait("scalar", "s")
                ctx.emit("scalar", lambda e: e.activation(
                    b2[:], w1[:], AF.Square), inc="s")
                sq_done = ctx.cnt["s"]
                ctx.wait("scalar", "v", ctx.cnt["v"])
                for c in range(4):
                    ctx.wait("tensor", "s", sq_done)
                    if c > 0:
                        ctx.wait("tensor", "s", ctx.cnt["s"])
                    ctx.emit("tensor", lambda e, c=c: e.matmul(
                        a_ps[:, 1024:1536], jj_bf[:],
                        b2[:, c * 512:(c + 1) * 512], start=True, stop=True),
                        inc="t")
                    ctx.wait("scalar", "t", ctx.cnt["t"])
                    if want_vhat:
                        ctx.emit("scalar", lambda e, c=c: e.activation(
                            w2[:, c * 512:(c + 1) * 512], a_ps[:, 1024:1536],
                            AF.Ln, bias=epsc[:, 2:3], scale=1.0), inc="s")
                    else:
                        ctx.emit("scalar", lambda e, c=c: e.activation(
                            w2[:, c * 512:(c + 1) * 512], a_ps[:, 1024:1536],
                            AF.Ln, bias=epsc[:, 1:2], scale=1.0 / HD), inc="s")
                    ctx.wait("scalar", "s")
                    ctx.emit("scalar", lambda e, c=c: e.activation(
                        rstdb[:, c * 512:(c + 1) * 512], w2[:, c * 512:(c + 1) * 512],
                        AF.Exp, bias=0.0, scale=-0.5), inc="s")
                rstd_done = ctx.cnt["s"]
                if TRUNC <= 2:
                    return ctx.cnt["v"]

                if want_vhat:
                    ctx.wait("vector", "s", rstd_done)
                    ctx.emit("vector", lambda e: e.tensor_tensor(
                        vhat[:], w1[:], rstdb[:], ALU.mult), inc="v")
                    ctx.emit("vector", lambda e: e.tensor_copy(b1[:], w1[:]),
                             inc="v")
                    vb_done = ctx.cnt["v"]
                    ctx.wait("tensor", "v", vb_done)
                    for grp in range(4):
                        half = (grp % 2) * 1024
                        if grp >= 2:
                            ctx.wait("tensor", "v", ctx.cnt["v"])
                        for j in range(4):
                            jt = grp * 4 + j
                            ctx.emit("tensor", lambda e, jt=jt, half=half, j=j:
                                     e.transpose(
                                         pt_ps[:, half + j * 128: half + (j + 1) * 128],
                                         b1[:, jt * 128:(jt + 1) * 128], id_bf[:]),
                                     inc="t")
                        ctx.wait("vector", "t", ctx.cnt["t"])
                        ctx.emit("vector", lambda e, grp=grp, half=half:
                                 e.tensor_copy(
                                     v_tok[:, grp * 512:(grp + 1) * 512],
                                     pt_ps[:, half:half + 512]), inc="v")
                    return ctx.cnt["v"]

                ctx.wait("sync", "s", rstd_done)
                ctx.dma("sync", w2[0:32, :], w1[32:64, :])
                swap_w = ctx.dma("sync", w2[32:64, :], w1[0:32, :])
                if TRUNC <= 3:
                    return ctx.cnt["v"]
                ctx.wait("vector", "s", rstd_done)
                ctx.wait("vector", "w", swap_w)
                ctx.emit("vector", lambda e: e.tensor_tensor(
                    rb[:], w1[0:64, :], c2_s[:], ALU.mult))
                ctx.emit("vector", lambda e: e.tensor_tensor(
                    rb2[:], w2[0:64, :], s2_s[:], ALU.mult), inc="v")
                ctx.wait("vector", "v")
                ctx.emit("vector", lambda e: e.tensor_tensor(
                    w1[0:64, :], rb[:], rb2[:], ALU.add), inc="v")
                ctx.wait("vector", "v")
                if is_q:
                    ctx.wait("vector", "v", gams_done)
                    ctx.emit("vector", lambda e, hq=hq: e.scalar_tensor_tensor(
                        dst_bf[:], w1[:], gams[:, hq:hq + 1], rstdb[:],
                        ALU.mult, ALU.mult), inc="v")
                else:
                    ctx.emit("vector", lambda e: e.tensor_tensor(
                        dst_bf[:], w1[:], rstdb[:], ALU.mult), inc="v")
                return ctx.cnt["v"]

            y_dma_done = [0]
            wcol_guard = [0]
            exp_ch_s = [0, 0, 0, 0]
            for h in range(HKV):
                k_done = build_row(w_k, h * 128, bk_s, h * 128, ka_b, False,
                                   None, k_bf)
                if DEV and dev_stop == "k0" and h == 0:
                    ctx.wait("gpsimd", "v", ctx.cnt["v"])
                    ctx.wait("gpsimd", "s", ctx.cnt["s"])
                    for nm, src in (("dev_w1", w1[:]), ("dev_rstdb", rstdb[:]),
                                    ("dev_k", k_bf[:]), ("dev_wcol", wcol_bf[:]),
                                    ("dev_ka", ka_b[:]), ("dev_sq", b2[:]),
                                    ("dev_w2", w2[:])):
                        if nm in dev:
                            ctx.dma("gpsimd", dev[nm], src)
                    _final_waits(ctx)
                    with nc.Block() as block:
                        ctx.finalize(block)
                    return nc
                v_done = build_row(w_v, h * 128, bv_s, h * 128, va_b, False,
                                   None, None, want_vhat=True)
                for g in range(4):
                    hq = h * 4 + g
                    q_done = build_row(w_q, hq * 128, bq_s, hq * 128, qa_b,
                                       True, hq, q_bf)
                    ctx.wait("vector", "gy")
                    ctx.wait("vector", "gy2")
                    pt_parity = [0]
                    pend_pt = [None]
                    pt_last_t = [0]

                    def emit_pt(piq, pbf_done):
                        ctx.wait("tensor", "v", pbf_done)
                        col = piq % 4
                        for jj in range((piq + 1 + 7) // 8):
                            cnt8 = min(8, piq + 1 - jj * 8)
                            par = pt_parity[0] % 2
                            pt_parity[0] += 1
                            if pt_parity[0] > 2:
                                ctx.wait("tensor", "v", ctx.cnt["v"])
                            for u in range(cnt8):
                                jt = jj * 8 + u
                                ctx.emit("tensor",
                                         lambda e, jt=jt, par=par, u=u:
                                         e.transpose(
                                             pt_ps[:, par * 1024 + u * 128:
                                                   par * 1024 + (u + 1) * 128],
                                             b1[:, jt * 128:(jt + 1) * 128],
                                             id_bf[:]), inc="t")
                            ctx.wait("vector", "t", ctx.cnt["t"])
                            ctx.emit("vector",
                                     lambda e, jj=jj, cnt8=cnt8, par=par,
                                     col=col: e.tensor_copy(
                                         pt_bf[:].rearrange(
                                             "p (j c) -> p j c", j=16)
                                         [:, jj * 8:jj * 8 + cnt8,
                                          col * 128:(col + 1) * 128],
                                         pt_ps[:, par * 1024:
                                               par * 1024 + cnt8 * 128]
                                         .rearrange("p (j c) -> p j c",
                                                    j=cnt8)), inc="v")
                        pt_last_t[0] = ctx.cnt["t"]

                    for G in range(4):
                        for iq in range(4 * G, 4 * G + 4):
                            W = (iq + 1) * 128
                            nch = (W + 511) // 512
                            ctx.wait("tensor", "v", q_done)
                            for c in range(nch):
                                n = min(512, W - c * 512)
                                last = (c == nch - 1)
                                ctx.wait("tensor", "s", exp_ch_s[c])
                                ctx.emit("tensor",
                                         lambda e, c=c, n=n, last=last, iq=iq:
                                         e.matmul(a_ps[:, c * 512:c * 512 + n],
                                                  q_bf[:, iq * 128:(iq + 1) * 128],
                                                  k_bf[:, c * 512:c * 512 + n],
                                                  start=True, stop=not last),
                                         inc=(None if last else "t"))
                                if not last:
                                    ctx.wait("vector", "t", ctx.cnt["t"])
                                    ctx.emit("vector", lambda e, c=c, n=n:
                                             e.reduce_max(
                                                 st[:, 80 + c:81 + c],
                                                 a_ps[:, c * 512:c * 512 + n],
                                                 axis=AX.X, negate=True),
                                             inc="v")
                            kd = iq % 4
                            cd = iq // 4
                            nd_ = W - cd * 512
                            ctx.emit("tensor", lambda e, kd=kd, cd=cd, nd_=nd_:
                                     e.matmul(
                                         a_ps[:, cd * 512:cd * 512 + nd_],
                                         id_bf[:],
                                         mask_bf[:, kd * 512:kd * 512 + nd_],
                                         start=False, stop=True), inc="t")
                            s_done = ctx.cnt["t"]
                            ctx.wait("vector", "t", s_done)
                            ctx.emit("vector", lambda e, cd=cd, nd_=nd_:
                                     e.reduce_max(
                                         st[:, 80 + cd:81 + cd],
                                         a_ps[:, cd * 512:cd * 512 + nd_],
                                         axis=AX.X, negate=True), inc="v")
                            ctx.wait("vector", "v")
                            if nch > 1:
                                ctx.emit("vector", lambda e, nch=nch:
                                         e.tensor_reduce(
                                             st[:, 86:87], st[:, 80:80 + nch],
                                             axis=AX.X, op=ALU.min), inc="v")
                                ctx.wait("vector", "v")
                                bias_col = st[:, 86:87]
                            else:
                                bias_col = st[:, 80:81]
                            ctx.wait("scalar", "v", ctx.cnt["v"])
                            nech = (W + 1023) // 1024
                            for c in range(nech):
                                n = min(1024, W - c * 1024)
                                ctx.emit("scalar", lambda e, c=c, n=n,
                                         bias_col=bias_col: e.activation(
                                             w2[:, c * 1024:c * 1024 + n],
                                             a_ps[:, c * 1024:c * 1024 + n],
                                             AF.Exp, bias=bias_col, scale=1.0,
                                             accum_out=st[:, 88 + c:89 + c]),
                                         inc="s")
                                exp_ch_s[2 * c] = ctx.cnt["s"]
                                if 2 * c + 1 < 4:
                                    exp_ch_s[2 * c + 1] = ctx.cnt["s"]
                            exp_done = ctx.cnt["s"]
                            if pend_pt[0] is not None:
                                emit_pt(*pend_pt[0])
                                pend_pt[0] = None
                            stop_at = 4 + ((hq * 16 + iq + 1) * 3) // 2
                            while p1_next < stop_at and p1_next < n_p1:
                                emit_pass1(p1_next); p1_next += 1
                            ctx.wait("vector", "s", exp_done)
                            if nech > 1:
                                ctx.emit("vector", lambda e, nech=nech:
                                         e.reduce_sum(
                                             st[:, 94:95], st[:, 88:88 + nech],
                                             axis=AX.X), inc="v")
                                ctx.wait("vector", "v")
                                den_col = st[:, 94:95]
                            else:
                                den_col = st[:, 88:89]
                            ctx.emit("vector", lambda e, den_col=den_col:
                                     e.reciprocal(st[:, 96:97], den_col),
                                     inc="v")
                            ctx.wait("vector", "v")
                            ctx.wait("vector", "t", pt_last_t[0])
                            ctx.emit("vector", lambda e, W=W: e.tensor_scalar(
                                b1[:, 0:W], w2[:, 0:W], st[:, 96:97], None,
                                ALU.mult), inc="v")
                            pend_pt[0] = (iq, ctx.cnt["v"])
                        if pend_pt[0] is not None:
                            emit_pt(*pend_pt[0])
                            pend_pt[0] = None
                        for j in range(4 * G + 1, 4 * G + 4):
                            sc = (j - 4 * G) * 128
                            ctx.emit("vector", lambda e, j=j, sc=sc: e.memset(
                                pt_bf[:, j * 512: j * 512 + sc], 0.0),
                                inc=("v" if j == 4 * G + 3 else None))
                        ctx.wait("vector", "v")
                        ctx.wait("tensor", "v", ctx.cnt["v"])
                        for j in range(4 * G + 4):
                            ctx.emit("tensor", lambda e, j=j, G=G: e.matmul(
                                y_ps[:],
                                v_tok[:, j * 128:(j + 1) * 128],
                                pt_bf[:, j * 512: (j + 1) * 512],
                                start=(j == 0), stop=(j == 4 * G + 3)),
                                inc=("t" if j == 4 * G + 3 else None))
                        av_done = ctx.cnt["t"]
                        ctx.wait("vector", "t", av_done)
                        ctx.emit("vector", lambda e, G=G: e.tensor_tensor(
                            b2[:, 0:512], y_ps[:],
                            vhat[:, G * 512:(G + 1) * 512], ALU.mult), inc="v")
                        ctx.wait("tensor", "v", ctx.cnt["v"])
                        ctx.emit("tensor", lambda e: e.matmul(
                            cb_ps[:], jj_bf[:], b2[:, 0:512],
                            start=True, stop=True), inc="t")
                        ctx.wait("vector", "t", ctx.cnt["t"])
                        ctx.emit("vector", lambda e, G=G: e.tensor_tensor(
                            w1[:, 0:512], cb_ps[:],
                            vhat[:, G * 512:(G + 1) * 512], ALU.mult), inc="v")
                        ctx.wait("vector", "v")
                        par = G % 2
                        gsem = ("gy", "gy2")[par]
                        if G >= 2:
                            ctx.wait("vector", gsem)
                        ctx.emit("vector", lambda e, par=par: e.tensor_tensor(
                            ybuf[:, par * 512:(par + 1) * 512], y_ps[:],
                            w1[:, 0:512], ALU.subtract), inc="v")
                        ctx.wait("gpsimd", "v", ctx.cnt["v"])
                        y_dma_done[0] = ctx.dma(
                            "gpsimd",
                            y_dram[hq * 128:(hq + 1) * 128, G * 512:(G + 1) * 512],
                            ybuf[:, par * 512:(par + 1) * 512], sem=gsem)
            while p1_next < n_p1:
                emit_pass1(p1_next); p1_next += 1
            y_all_g = ctx.cnt["g"]
            pass1_done_s = ctx.cnt["s"]

            if DEV:
                if "dev_k" in dev:
                    ctx.wait("gpsimd", "v", ctx.cnt["v"])
                    ctx.dma("gpsimd", dev["dev_k"], k_bf[:])
                if "dev_q" in dev:
                    ctx.wait("gpsimd", "v", ctx.cnt["v"])
                    ctx.dma("gpsimd", dev["dev_q"], q_bf[:])
                if "dev_vhat" in dev:
                    ctx.wait("gpsimd", "v", ctx.cnt["v"])
                    ctx.dma("gpsimd", dev["dev_vhat"], vhat[:])
                if "dev_vtok" in dev:
                    ctx.wait("gpsimd", "v", ctx.cnt["v"])
                    ctx.dma("gpsimd", dev["dev_vtok"], v_tok[:])
                if "dev_y" in dev:
                    ctx.wait("gpsimd", "g", y_all_g)
                    ctx.dma("gpsimd", dev["dev_y"], y_dram)

            if dev_stop in ("kv", "attn"):
                _final_waits(ctx)
                with nc.Block() as block:
                    ctx.finalize(block)
                return nc

        ctx.phase_barrier()
        with ExitStack() as late:
            sb3 = lambda n, s, d: late.enter_context(nc.sbuf_tensor(n, s, d))
            slabB_h = sb3("slabB", [128, ND * T], BF16)
            slabB = slabB_h[:].rearrange("p (dt t) -> p dt t", dt=ND)
            wproj_v = slabA

            with ExitStack() as p3s:
                sb = lambda n, s, d: p3s.enter_context(nc.sbuf_tensor(n, s, d))
                ycol0 = sb("ycol0", [128, 2048], BF16)
                ycol1 = sb("ycol1", [128, 2048], BF16)
                xt20 = sb("xt20", [128, 2048], F32)
                xt21 = sb("xt21", [128, 2048], F32)
                ap2_s = sb("ap2_s", [128, 256], BF16)
                pa_bf = sb("pa_bf", [16, 2048], BF16)
                pb_s = sb("pb_s", [16, 2048], BF16)
                pr_ps = p3s.enter_context(nc.psum_tensor("pr_ps", [128, 2048], F32))
                pp_ps = p3s.enter_context(nc.psum_tensor("pp_ps", [128, 512], F32))
                ptr2 = p3s.enter_context(nc.psum_tensor("ptr2", [128, 2048], BF16))

                for dt in range(ND):
                    ctx.dma("gpsimd", slabA[:, dt, :],
                            w_p[dt * 128:(dt + 1) * 128, :])
                ctx.dma("gpsimd", ap2_s[:].rearrange("p (dt r) -> p dt r", dt=ND),
                        a_p.rearrange("(dt p) r -> p dt r", p=128))
                ctx.dma("gpsimd", pb_s[:], b_p)
                g_p3 = ctx.cnt["g"]

                evac_v = [0, 0]
                mm_t = [0, 0]
                xm_dma = [0]
                tpose2 = [0]
                for tt in range(NT):
                    par = tt % 2
                    ycol = (ycol0, ycol1)[par]
                    xt2 = (xt20, xt21)[par]
                    ysem = ("wy", "wy2")[par]
                    xsem = ("wx", "wx2")[par]
                    if tt >= 2:
                        ctx.wait("sync", "t", mm_t[par])
                    ctx.dma("sync", ycol[:].rearrange("p (dt c) -> p dt c", dt=ND),
                            y_dram[:, tt * 128:(tt + 1) * 128]
                            .rearrange("(dt p) c -> p dt c", p=128), sem=ysem)
                    if tt >= 2:
                        ctx.wait("sync", "v", evac_v[par])
                    ctx.dma("sync", xt2[:], xb[tt * 128:(tt + 1) * 128, :], sem=xsem)
                    ctx.wait("tensor", ysem)
                    if tt == 0:
                        ctx.wait("tensor", "g", g_p3)
                    if tt >= 1:
                        ctx.wait("tensor", "s", ctx.cnt["s"])
                    for dt in range(ND):
                        ctx.emit("tensor", lambda e, dt=dt, ycol=ycol: e.matmul(
                            pp_ps[0:16, 0:128], ap2_s[:, dt * 16:(dt + 1) * 16],
                            ycol[:, dt * 128:(dt + 1) * 128],
                            start=(dt == 0), stop=(dt == ND - 1)),
                            inc=("t" if dt == ND - 1 else None))
                    ctx.wait("scalar", "t", ctx.cnt["t"])
                    ctx.emit("scalar", lambda e, tt=tt: e.copy(
                        pa_bf[0:16, tt * 128:(tt + 1) * 128], pp_ps[0:16, 0:128]),
                        inc="s")
                    pa_done = ctx.cnt["s"]
                    if tt >= 1:
                        ctx.wait("tensor", "v", evac_v[1 - par])
                    for dq in range(4):
                        for dt in range(ND):
                            ctx.emit("tensor",
                                     lambda e, dq=dq, dt=dt, ycol=ycol: e.matmul(
                                         pr_ps[:, dq * 512:(dq + 1) * 512],
                                         ycol[:, dt * 128:(dt + 1) * 128],
                                         wproj_v[:, dt, dq * 512:(dq + 1) * 512],
                                         start=(dt == 0), stop=False))
                        if dq == 0:
                            ctx.wait("tensor", "s", pa_done)
                        ctx.emit("tensor", lambda e, dq=dq, tt=tt: e.matmul(
                            pr_ps[:, dq * 512:(dq + 1) * 512],
                            pa_bf[0:16, tt * 128:(tt + 1) * 128],
                            pb_s[0:16, dq * 512:(dq + 1) * 512],
                            start=False, stop=True),
                            inc=("t" if dq == 3 else None))
                    mm_t[par] = ctx.cnt["t"]
                    ctx.wait("vector", "t", mm_t[par])
                    ctx.wait("vector", xsem)
                    if tt >= 1:
                        ctx.wait("vector", "gm")
                    for dq in range(4):
                        ctx.emit("vector", lambda e, dq=dq, xt2=xt2: e.tensor_tensor(
                            w1[:, dq * 512:(dq + 1) * 512],
                            pr_ps[:, dq * 512:(dq + 1) * 512],
                            xt2[:, dq * 512:(dq + 1) * 512], ALU.add),
                            inc=("v" if dq == 3 else None))
                    evac_v[par] = ctx.cnt["v"]
                    ctx.wait("gpsimd", "v", evac_v[par])
                    xm_dma[0] = ctx.dma("gpsimd", xmid_dram[tt * 128:(tt + 1) * 128, :],
                                        w1[:], sem="gm")
                    ctx.wait("vector", "v")
                    ctx.emit("vector", lambda e: e.reduce_sum(
                        st[:, 0:1], w1[:], axis=AX.X), inc="v")
                    ctx.wait("vector", "v")
                    ctx.wait("scalar", "v", ctx.cnt["v"])
                    ctx.emit("scalar", lambda e: e.activation(
                        b2[:], w1[:], AF.Square, accum_out=st[:, 8:9]), inc="s")
                    ctx.wait("vector", "s", ctx.cnt["s"])
                    ctx.emit("vector", lambda e: e.tensor_scalar(
                        st[:, 16:17], st[:, 0:1], 1.0 / D, None, ALU.mult))
                    ctx.emit("vector", lambda e: e.tensor_scalar(
                        st[:, 24:25], st[:, 8:9], 1.0 / D, None, ALU.mult), inc="v")
                    ctx.wait("vector", "v")
                    ctx.emit("vector", lambda e: e.scalar_tensor_tensor(
                        st[:, 32:33], st[:, 16:17], -1.0, st[:, 16:17],
                        ALU.mult, ALU.mult), inc="v")
                    ctx.wait("vector", "v")
                    ctx.emit("vector", lambda e: e.tensor_tensor(
                        st[:, 40:41], st[:, 24:25], st[:, 32:33], ALU.add), inc="v")
                    ctx.wait("scalar", "v", ctx.cnt["v"])
                    ctx.emit("scalar", lambda e: e.activation(
                        st[:, 48:49], st[:, 40:41], AF.Ln, bias=epsc[:, 0:1],
                        scale=1.0), inc="s")
                    ctx.wait("scalar", "s")
                    ctx.emit("scalar", lambda e: e.activation(
                        st[:, 56:57], st[:, 48:49], AF.Exp, bias=0.0, scale=-0.5),
                        inc="s")
                    ctx.wait("vector", "s", ctx.cnt["s"])
                    ctx.emit("vector", lambda e: e.scalar_tensor_tensor(
                        st[:, 64:65], st[:, 16:17], -1.0, st[:, 56:57],
                        ALU.mult, ALU.mult), inc="v")
                    ctx.wait("scalar", "v", ctx.cnt["v"])
                    if tt >= 1:
                        ctx.wait("scalar", "t", tpose2[0])
                    ctx.emit("scalar", lambda e: e.activation(
                        b1[:], w1[:], AF.Identity, bias=st[:, 64:65],
                        scale=st[:, 56:57]), inc="s")
                    norm2 = ctx.cnt["s"]
                    ctx.wait("tensor", "s", norm2)
                    for grp in range(4):
                        half = (grp % 2) * 1024
                        if grp >= 2:
                            ctx.wait("tensor", "v", ctx.cnt["v"])
                        for j in range(4):
                            dt = grp * 4 + j
                            ctx.emit("tensor", lambda e, dt=dt, half=half, j=j:
                                     e.transpose(
                                         ptr2[:, half + j * 128: half + (j + 1) * 128],
                                         b1[:, dt * 128:(dt + 1) * 128], id_bf[:]),
                                     inc="t")
                        ctx.wait("vector", "t", ctx.cnt["t"])
                        ctx.emit("vector", lambda e, grp=grp, half=half, tt=tt:
                                 e.tensor_copy(
                                     slabB[:, grp * 4:(grp + 1) * 4,
                                           tt * 128:(tt + 1) * 128],
                                     ptr2[:, half:half + 512]
                                     .rearrange("p (j c) -> p j c", j=4)), inc="v")
                    tpose2[0] = ctx.cnt["t"]

            if DEV and dev_stop == "proj":
                ctx.wait("gpsimd", "v", ctx.cnt["v"])
                ctx.wait("gpsimd", "s", ctx.cnt["s"])
                ctx.wait("gpsimd", "gm")
                if "dev_xmid" in dev:
                    ctx.dma("gpsimd", dev["dev_xmid"], xmid_dram)
                if "dev_xn2" in dev:
                    ctx.dma("gpsimd", dev["dev_xn2"], slabB_h[:])
                _final_waits(ctx)
                with nc.Block() as block:
                    ctx.finalize(block)
                return nc

            ctx.phase_barrier()

            def finalize_scale(mi, nelem):
                ctx.emit("vector", lambda e, mi=mi: e.reduce_sum(
                    ptmp[:, 0:16], pacc[:, mi * 128:(mi + 1) * 128]
                    .rearrange("p (a b) -> p a b", a=16), axis=AX.X), inc="v")
                ctx.wait("vector", "v")
                ctx.emit("vector", lambda e: e.reduce_sum(
                    st[:, 8:9], ptmp[:, 0:16], axis=AX.X), inc="v")
                ctx.wait("vector", "v")
                ctx.wait("tensor", "v", ctx.cnt["v"])
                ctx.emit("tensor", lambda e: e.transpose(
                    pp_ps2[0:1, 0:128], st[:, 8:9], id_f[:]), inc="t")
                ctx.wait("vector", "t", ctx.cnt["t"])
                ctx.emit("vector", lambda e: e.tensor_copy(
                    prow[:], pp_ps2[0:1, 0:128]), inc="v")
                ctx.wait("vector", "v")
                ctx.emit("vector", lambda e: e.tensor_tensor_scan(
                    srow[:], prow[:], zrow[:], 0.0, ALU.add, ALU.add), inc="v")
                ctx.wait("vector", "v")
                ctx.emit("vector", lambda e: e.tensor_scalar(
                    st[0:1, 16:17], srow[0:1, 127:128],
                    5.9604644775390625e-08, 1e-8, ALU.mult, ALU.add), inc="v")
                ctx.wait("vector", "v")
                ctx.wait("tensor", "v", ctx.cnt["v"])
                ctx.emit("tensor", lambda e: e.matmul(
                    pp_ps2[:, 0:1], onesr_f[:], st[0:1, 16:17],
                    start=True, stop=True), inc="t")
                ctx.wait("vector", "t", ctx.cnt["t"])
                ctx.emit("vector", lambda e, mi=mi: e.tensor_copy(
                    scl[:, 16 * mi:16 * mi + 1], pp_ps2[:, 0:1]), inc="v")
                ctx.wait("vector", "v")
                ctx.emit("vector", lambda e, mi=mi: e.tensor_tensor(
                    scl[:, 16 * mi + 4:16 * mi + 5], scl[:, 16 * mi:16 * mi + 1],
                    scl[:, 48:49], ALU.mult), inc="v")
                ctx.wait("vector", "v")
                ctx.emit("vector", lambda e, mi=mi: e.tensor_scalar(
                    scl[:, 16 * mi + 12:16 * mi + 13],
                    scl[:, 16 * mi:16 * mi + 1], -1.0, None, ALU.mult), inc="v")
                ctx.wait("vector", "v")

            with ExitStack() as p35:
                sb = lambda n, s, d: p35.enter_context(nc.sbuf_tensor(n, s, d))
                wd0 = sb("wd0", [128, 2048], F32)
                wd1 = sb("wd1", [128, 2048], F32)
                tds = sb("tds", [128, 2048], BF16)
                pp_ps2 = p35.enter_context(nc.psum_tensor("pp_ps2", [128, 512], F32))
                tp_ps = p35.enter_context(nc.psum_tensor("tp_ps", [128, 2048], BF16))

                for mi, nel in ((0, DE * D), (1, DE * D), (2, DE * D)):
                    finalize_scale(mi, nel)
                ctx.emit("vector", lambda e: e.tensor_tensor(
                    scl[:, 56:57], scl[:, 32:33], scl[:, 0:1], ALU.mult), inc="v")
                ctx.wait("vector", "v")
                scl_done_v = ctx.cnt["v"]

                qd_par = [0, 0]
                for ci in range(64):
                    r = ci // 4
                    ec = ci % 4
                    par = ci % 2
                    wd = (wd0, wd1)[par]
                    wsm = ("u0", "u1")[par]
                    if ci >= 2:
                        ctx.wait("sync", "v", qd_par[par])
                    ctx.dma("sync", wd[:],
                            w_down[r * 128:(r + 1) * 128, ec * 2048:(ec + 1) * 2048],
                            sem=wsm)
                    ctx.wait("vector", wsm)
                    if ci == 0:
                        ctx.wait("vector", "v", scl_done_v)
                    ctx.emit("vector", lambda e, wd=wd: e.tensor_scalar(
                        w2[:], wd[:], scl[:, 60:61], None, ALU.mult), inc="v")
                    ctx.wait("vector", "v")
                    ctx.emit("vector", lambda e: e.tensor_scalar(
                        w1[:], w2[:], scl[:, 4:5], None, ALU.add), inc="v")
                    ctx.wait("vector", "v")
                    ctx.emit("vector", lambda e, wd=wd: e.tensor_scalar(
                        wd[:], w1[:], scl[:, 12:13], None, ALU.is_lt), inc="v")
                    ctx.wait("vector", "v")
                    ctx.emit("vector", lambda e: e.tensor_scalar(
                        w1[:], w2[:], scl[:, 4:5], None, ALU.subtract), inc="v")
                    ctx.wait("vector", "v")
                    if ci >= 1:
                        ctx.wait("vector", "t", ctx.cnt["t"])
                    ctx.emit("vector", lambda e, wd=wd: e.scalar_tensor_tensor(
                        b1[:], w1[:], scl[:, 0:1], wd[:],
                        ALU.is_gt, ALU.subtract), inc="v")
                    qd_par[par] = ctx.cnt["v"]
                    ctx.wait("tensor", "v", ctx.cnt["v"])
                    if ci >= 1:
                        ctx.wait("tensor", "gt2")
                    for grp in range(4):
                        half = (grp % 2) * 1024
                        if grp >= 2:
                            ctx.wait("tensor", "s", ctx.cnt["s"])
                        for j in range(4):
                            et = grp * 4 + j
                            ctx.emit("tensor", lambda e, et=et, half=half, j=j:
                                     e.transpose(
                                         tp_ps[:, half + j * 128: half + (j + 1) * 128],
                                         b1[:, et * 128:(et + 1) * 128], id_bf[:]),
                                     inc="t")
                        ctx.wait("scalar", "t", ctx.cnt["t"])
                        ctx.emit("scalar", lambda e, grp=grp, half=half:
                                 e.copy(
                                     tds[:, grp * 512:(grp + 1) * 512],
                                     tp_ps[:, half:half + 512]), inc="s")
                    ctx.wait("gpsimd", "s", ctx.cnt["s"])
                    ctx.dma(
                        "gpsimd",
                        tdT_dram[ec * 2048:(ec + 1) * 2048, r * 128:(r + 1) * 128]
                        .rearrange("(et p) c -> p et c", p=128),
                        tds[:].rearrange("p (et c) -> p et c", et=16), sem="gt2")

            if DEV and dev_stop == "wd":
                ctx.wait("gpsimd", "gt2")
                if "dev_tdT" in dev:
                    ctx.dma("gpsimd", dev["dev_tdT"], tdT_dram)
                if "dev_scales" in dev:
                    ctx.wait("gpsimd", "v", ctx.cnt["v"])
                    ctx.dma("gpsimd", dev["dev_scales"], scl[:, 0:64])
                _final_waits(ctx)
                with nc.Block() as block:
                    ctx.finalize(block)
                return nc

            ctx.phase_barrier()
            with ExitStack() as p4s:
                sb = lambda n, s, d: p4s.enter_context(nc.sbuf_tensor(n, s, d))
                wu0 = sb("wu0", [128, 2048], F32)
                wu1 = sb("wu1", [128, 2048], F32)
                wg0 = sb("wg0", [128, 2048], F32)
                wg1 = sb("wg1", [128, 2048], F32)
                qa = sb("qa", [128, 1024], F32)
                qd = sb("qd", [128, 1024], F32)
                up_ps = p4s.enter_context(nc.psum_tensor("up_ps", [128, 1024], F32))
                gt_ps = p4s.enter_context(nc.psum_tensor("gt_ps", [128, 2048], F32))
                tq_ps = p4s.enter_context(nc.psum_tensor("tq_ps", [128, 2048], BF16))
                ttuA = [slabA_h[:, 0:2048], slabA_h[:, 2048:4096]]
                ttgA = [slabA_h[:, 4096:6144], slabA_h[:, 6144:8192]]
                b1uA = [slabA_h[:, 8192:10240], slabA_h[:, 10240:12288]]
                b1gA = [slabA_h[:, 12288:14336], slabA_h[:, 14336:16384]]
                wsil = [w1, w2]

                def quant_cols(src, b1dst, sname, c0):
                    cs = slice(c0, c0 + 1024)
                    ctx.emit("vector", lambda e, src=src, cs=cs: e.tensor_scalar(
                        qa[:], src[:, cs], scl[:, 60:61], None, ALU.mult),
                        inc="v")
                    ctx.wait("vector", "v")
                    ctx.emit("vector", lambda e, sname=sname: e.tensor_scalar(
                        qd[:], qa[:], scl[:, sname + 4:sname + 5], None,
                        ALU.add), inc="v")
                    ctx.wait("vector", "v")
                    ctx.emit("vector", lambda e, src=src, cs=cs, sname=sname:
                             e.tensor_scalar(
                                 src[:, cs], qd[:],
                                 scl[:, sname + 12:sname + 13], None,
                                 ALU.is_lt), inc="v")
                    ctx.wait("vector", "v")
                    ctx.emit("vector", lambda e, sname=sname: e.tensor_scalar(
                        qd[:], qa[:], scl[:, sname + 4:sname + 5], None,
                        ALU.subtract), inc="v")
                    ctx.wait("vector", "v")
                    ctx.emit("vector", lambda e, src=src, cs=cs, b1dst=b1dst,
                             sname=sname: e.scalar_tensor_tensor(
                                 b1dst[:, cs], qd[:],
                                 scl[:, sname:sname + 1], src[:, cs],
                                 ALU.is_gt, ALU.subtract), inc="v")
                    ctx.wait("vector", "v")

                tph_s = [0, 0]

                def tpose_evac(b1src, dst):
                    for grp in range(4):
                        h = grp % 2
                        half = h * 1024
                        ctx.wait("tensor", "s", tph_s[h])
                        for j in range(4):
                            dt = grp * 4 + j
                            ctx.emit("tensor", lambda e, dt=dt, half=half, j=j,
                                     b1src=b1src: e.transpose(
                                         tq_ps[:, half + j * 128:
                                               half + (j + 1) * 128],
                                         b1src[:, dt * 128:(dt + 1) * 128],
                                         id_bf[:]), inc="t")
                        ctx.wait("scalar", "t", ctx.cnt["t"])
                        ctx.emit("scalar", lambda e, grp=grp, half=half,
                                 dst=dst: e.copy(
                                     dst[:, grp * 512:(grp + 1) * 512],
                                     tq_ps[:, half:half + 512]), inc="s")
                        tph_s[h] = ctx.cnt["s"]

                qv_par = [0, 0]
                tpu_par = [0, 0]
                tpg_par = [0, 0]
                prev = [None]
                prev_silu = [0]

                def emit_hidden(st8):
                    j, silu_s, mmt, wsl = st8
                    ctx.wait("vector", "gh")
                    ctx.wait("vector", "s", silu_s)
                    ctx.wait("vector", "t", mmt)
                    for tc in range(4):
                        ctx.emit("vector", lambda e, tc=tc, wsl=wsl:
                                 e.tensor_tensor(
                                     b2[:, tc * 512:(tc + 1) * 512],
                                     wsl[:, tc * 512:(tc + 1) * 512],
                                     gt_ps[:, tc * 512:(tc + 1) * 512],
                                     ALU.mult), inc="v")
                    ctx.wait("gpsimd", "v", ctx.cnt["v"])
                    ctx.dma("gpsimd", hid_dram[j * 128:(j + 1) * 128, :],
                            b2[:], sem="gh")

                for ec in range(64):
                    par = ec % 2
                    wu = (wu0, wu1)[par]
                    wg = (wg0, wg1)[par]
                    su, sg = ("u0", "u1")[par], ("u2", "u3")[par]
                    if ec >= 2:
                        ctx.wait("sync", "v", qv_par[par])
                    ctx.dma("sync", wu[:], w_up[ec * 128:(ec + 1) * 128, :],
                            sem=su)
                    ctx.dma("sync", wg[:], w_gate[ec * 128:(ec + 1) * 128, :],
                            sem=sg)
                    ctx.wait("vector", su)
                    if ec >= 2:
                        ctx.wait("vector", "t", tpu_par[par])
                    quant_cols(wu, b1uA[par], 16, 0)
                    quant_cols(wu, b1uA[par], 16, 1024)
                    qu_v = ctx.cnt["v"]
                    ctx.wait("vector", sg)
                    if ec >= 2:
                        ctx.wait("vector", "t", tpg_par[par])
                    quant_cols(wg, b1gA[par], 32, 0)
                    quant_cols(wg, b1gA[par], 32, 1024)
                    qv_par[par] = ctx.cnt["v"]
                    qg_v = ctx.cnt["v"]
                    if prev[0] is not None:
                        emit_hidden(prev[0])
                    hid_v = ctx.cnt["v"]
                    ctx.wait("tensor", "v", qu_v)
                    tpose_evac(b1uA[par], ttuA[par])
                    tpu_par[par] = ctx.cnt["t"]
                    ctx.wait("tensor", "v", qg_v)
                    tpose_evac(b1gA[par], ttgA[par])
                    tpg_par[par] = ctx.cnt["t"]
                    ctx.wait("tensor", "s", ctx.cnt["s"])
                    for pair in range(2):
                        if pair == 0 and prev[0] is not None:
                            ctx.wait("tensor", "s", prev_silu[0])
                        if pair == 1:
                            ctx.wait("tensor", "s", ctx.cnt["s"])
                        for dt in range(ND):
                            for sub in range(2):
                                tc = pair * 2 + sub
                                ctx.emit("tensor", lambda e, dt=dt, sub=sub,
                                         tc=tc, par=par: e.matmul(
                                             up_ps[:, sub * 512:(sub + 1) * 512],
                                             ttuA[par][:, dt * 128:(dt + 1) * 128],
                                             slabB[:, dt, tc * 512:(tc + 1) * 512],
                                             start=(dt == 0), stop=(dt == ND - 1)),
                                         inc=("t" if dt == ND - 1 and sub == 1
                                              else None))
                        up_t = ctx.cnt["t"]
                        if pair == 0 and prev[0] is not None:
                            ctx.wait("tensor", "v", hid_v)
                        for dt in range(ND):
                            for sub in range(2):
                                tc = pair * 2 + sub
                                ctx.emit("tensor", lambda e, dt=dt, tc=tc,
                                         par=par: e.matmul(
                                             gt_ps[:, tc * 512:(tc + 1) * 512],
                                             ttgA[par][:, dt * 128:(dt + 1) * 128],
                                             slabB[:, dt, tc * 512:(tc + 1) * 512],
                                             start=(dt == 0), stop=(dt == ND - 1)),
                                         inc=("t" if dt == ND - 1 and sub == 1
                                              else None))
                        ctx.wait("scalar", "t", up_t)
                        if pair == 0 and prev[0] is not None:
                            ctx.wait("scalar", "v", hid_v)
                        for sub in range(2):
                            tc = pair * 2 + sub
                            ctx.emit("scalar", lambda e, sub=sub, tc=tc,
                                     par=par: e.activation(
                                         wsil[par][:, tc * 512:(tc + 1) * 512],
                                         up_ps[:, sub * 512:(sub + 1) * 512],
                                         AF.Silu, scale=scl[:, 16:17]), inc="s")
                    prev_silu[0] = ctx.cnt["s"]
                    prev[0] = (ec, ctx.cnt["s"], ctx.cnt["t"], wsil[par])
                emit_hidden(prev[0])

            if DEV and dev_stop == "ffn_a":
                ctx.wait("gpsimd", "gh")
                if "dev_hid" in dev:
                    ctx.dma("gpsimd", dev["dev_hid"], hid_dram)
                _final_waits(ctx)
                with nc.Block() as block:
                    ctx.finalize(block)
                return nc

            ctx.phase_barrier()
            with ExitStack() as p5s:
                sb = lambda n, s, d: p5s.enter_context(nc.sbuf_tensor(n, s, d))
                td0 = sb("td0", [128, 512], BF16)
                td1 = sb("td1", [128, 512], BF16)
                xm0 = sb("xm0", [128, 512], F32)
                xm1 = sb("xm1", [128, 512], F32)
                ost0 = sb("ost0", [128, 512], F32)
                ost1 = sb("ost1", [128, 512], F32)
                po_a = p5s.enter_context(nc.psum_tensor("po_a", [128, 2048], F32))
                po_b = p5s.enter_context(nc.psum_tensor("po_b", [128, 2048], F32))
                slabAf = slabA_h[:]
                slabBf = slabB_h[:]

                def hid_tile(et, tslot):
                    base = (et % 32) * 1024 + tslot * 128
                    src = slabAf if et < 32 else slabBf
                    return src[:, base:base + 128]

                for tc in range(2):
                    if tc == 0:
                        ctx.phase_barrier()
                        for part in range(8):
                            e0 = part * 8
                            dst = slabAf if e0 < 32 else slabBf
                            ctx.dma("sync",
                                    dst[:, (e0 % 32) * 1024:((e0 % 32) + 8) * 1024]
                                    .rearrange("p (et c) -> p et c", et=8),
                                    hid_dram[e0 * 128:(e0 + 8) * 128, 0:1024]
                                    .rearrange("(et p) c -> p et c", p=128),
                                    sem="wh")
                    ctx.wait("tensor", "wh")
                    if tc == 1:
                        ctx.wait("tensor", "gm")
                    for dq in range(4):
                        tb_par = [0, 0]
                        for et in range(64):
                            par = et % 2
                            td = (td0, td1)[par]
                            tq = ("sync", "gpsimd")[par]
                            tsm = ("u0", "ga")[par]
                            if et >= 2:
                                ctx.wait(tq, "t", tb_par[par])
                            ctx.dma(tq, td[:],
                                    tdT_dram[et * 128:(et + 1) * 128,
                                             dq * 512:(dq + 1) * 512], sem=tsm)
                            ctx.wait("tensor", tsm)
                            if et == 0 and (dq > 0 or tc > 0):
                                ctx.wait("tensor", "v", ctx.cnt["v"])
                            for tslot in range(8):
                                bank = po_a if tslot < 4 else po_b
                                boff = (tslot % 4) * 512
                                ctx.emit("tensor",
                                         lambda e, et=et, tslot=tslot, bank=bank,
                                         boff=boff, td=td: e.matmul(
                                             bank[:, boff:boff + 512],
                                             hid_tile(et, tslot), td[:],
                                             start=(et == 0), stop=(et == 63)),
                                         inc=("t" if tslot == 7 else None))
                            tb_par[par] = ctx.cnt["t"]
                            if tc == 0 and dq == 3 and et % 8 == 7:
                                e0 = et - 7
                                ctx.wait("gpsimd", "t", ctx.cnt["t"])
                                dst = slabAf if e0 < 32 else slabBf
                                ctx.dma("gpsimd",
                                        dst[:, (e0 % 32) * 1024:
                                            ((e0 % 32) + 8) * 1024]
                                        .rearrange("p (et c) -> p et c", et=8),
                                        hid_dram[e0 * 128:(e0 + 8) * 128,
                                                 1024:2048]
                                        .rearrange("(et p) c -> p et c", p=128),
                                        sem="gm")
                        dn_t = ctx.cnt["t"]
                        for tslot in range(8):
                            par = tslot % 2
                            xm = (xm0, xm1)[par]
                            ost = (ost0, ost1)[par]
                            xsm = ("u2", "u3")[par]
                            r0 = tc * 1024 + tslot * 128
                            if tslot >= 2:
                                ctx.wait("sync", "v", ctx.cnt["v"])
                            ctx.dma("sync", xm[:],
                                    xmid_dram[r0:r0 + 128, dq * 512:(dq + 1) * 512],
                                    sem=xsm)
                            ctx.wait("vector", xsm)
                            ctx.wait("vector", "t", dn_t)
                            osem = ("go", "go2")[par]
                            ctx.wait("vector", osem)
                            bank = po_a if tslot < 4 else po_b
                            boff = (tslot % 4) * 512
                            ctx.emit("vector", lambda e, bank=bank, boff=boff,
                                     xm=xm, ost=ost: e.scalar_tensor_tensor(
                                         ost[:], bank[:, boff:boff + 512],
                                         scl[:, 56:57], xm[:], ALU.mult, ALU.add),
                                     inc="v")
                            ctx.wait("gpsimd", "v", ctx.cnt["v"])
                            ctx.dma("gpsimd", out[r0:r0 + 128,
                                                  dq * 512:(dq + 1) * 512],
                                    ost[:], sem=osem)

        _final_waits(ctx)
        with nc.Block() as block:
            ctx.finalize(block)
    return nc


def _final_waits(ctx):
    for sem, v in ctx.cnt.items():
        if v:
            ctx.wait("gpsimd", sem, v)
            ctx.wait("sync", sem, v)


def _in_map(inputs, c):
    return {
        "xb": np.ascontiguousarray(inputs["L0_attn_x"][c]),
        "Wq": inputs["L0_attn_Wq"], "Wk": inputs["L0_attn_Wk"],
        "Wv": inputs["L0_attn_Wv"], "Wproj": inputs["L0_attn_Wproj"],
        "q_gain": np.ascontiguousarray(inputs["L0_attn_q_gain"].reshape(1, H)),
        "q_A": inputs["L0_attn_q_A"], "q_B": inputs["L0_attn_q_B"],
        "k_A": inputs["L0_attn_k_A"], "k_B": inputs["L0_attn_k_B"],
        "v_A": inputs["L0_attn_v_A"], "v_B": inputs["L0_attn_v_B"],
        "p_A": inputs["L0_attn_p_A"], "p_B": inputs["L0_attn_p_B"],
        "W_up": inputs["L0_ffn_W_up"], "W_gate": inputs["L0_ffn_W_gate"],
        "W_down": inputs["L0_ffn_W_down"],
    }


def kernel(**inputs):
    nc = build_program()
    in_maps = [_in_map(inputs, c) for c in range(B)]
    res = run_bass_kernel_spmd(nc, in_maps, core_ids=list(range(B)))
    return np.stack([res.results[c]["out"] for c in range(B)], axis=0)


# version 10
# speedup vs baseline: 1.0867x; 1.0867x over previous
import numpy as np
from contextlib import ExitStack
import concourse.bass as bass
import concourse.mybir as mybir
from concourse.bass_utils import run_bass_kernel_spmd

F32 = mybir.dt.float32
BF16 = mybir.dt.bfloat16
AF = mybir.ActivationFunctionType
ALU = mybir.AluOpType
AX = mybir.AxisListType

B, T, D = 8, 2048, 2048
H, HKV, HD = 16, 4, 128
KV = HKV * HD
RANK = 16
DE = 8192
NT = T // 128
ND = D // 128
EPS_LN = 1e-5
EPS_RMS = float(np.finfo(np.float32).eps)
SCALE_QK = float(HD ** -0.5)

DEV = False
SKIP_LORA_B = False
TRUNC = 99


class Ctx:
    ENGS = ("sync", "gpsimd", "tensor", "vector", "scalar")

    def __init__(self, nc):
        self.nc = nc
        self.ops = {e: [] for e in self.ENGS}
        self.cnt = {}
        self.h = {}

    def new_sem(self, name, stack):
        self.h[name] = stack.enter_context(self.nc.semaphore(name))
        self.cnt[name] = 0

    def wait(self, eng, sem, tgt=None):
        if tgt is None:
            tgt = self.cnt[sem]
        assert tgt <= self.cnt[sem], f"wait on future inc: {sem} {tgt} > {self.cnt[sem]}"
        h = self.h[sem]
        self.ops[eng].append(lambda e, h=h, tgt=tgt: e.wait_ge(h, tgt))

    def emit(self, eng, fn, inc=None, inc_amt=1):
        if inc is not None:
            hh = self.h[inc]
            self.cnt[inc] += inc_amt
            self.ops[eng].append(
                lambda e, fn=fn, hh=hh, a=inc_amt: fn(e).then_inc(hh, a))
        else:
            self.ops[eng].append(fn)
        return self.cnt.get(inc)

    def dma(self, eng, out, in_, sem=None):
        if sem is None:
            sem = "w" if eng == "sync" else "g"
        return self.emit(eng, lambda e, o=out, i=in_: e.dma_start(out=o, in_=i),
                         inc=sem, inc_amt=16)

    def phase_barrier(self):
        for eng in self.ENGS:
            for sem in self.cnt:
                if self.cnt[sem] > 0:
                    self.wait(eng, sem, self.cnt[sem])

    def finalize(self, block):
        for eng in self.ENGS:
            def run(e, ops=self.ops[eng]):
                for f in ops:
                    f(e)
            getattr(block, eng)(run)


def make_consts():
    inv_freq = (1.0 / (np.float32(10000.0) **
                       (np.arange(0, 64, 2, dtype=np.float32) / np.float32(64.0))))
    freqs = np.arange(T, dtype=np.float32)[:, None] * inv_freq[None, :]
    cosT = np.cos(freqs).astype(np.float32).T.copy()
    sinT = np.sin(freqs).astype(np.float32).T.copy()
    c2 = np.concatenate([cosT, cosT], axis=0)
    s2 = np.concatenate([sinT, -sinT], axis=0)
    ident = np.eye(128, dtype=np.float32)
    perm = np.zeros((128, 128), dtype=np.float32)
    perm[np.arange(32), np.arange(32) + 32] = 1.0
    perm[np.arange(32) + 32, np.arange(32)] = 1.0
    for i in range(64, 128):
        perm[i, i] = 1.0
    tri = np.where(np.arange(128)[None, :] <= np.arange(128)[:, None],
                   np.float32(0.0), np.float32(-1e30))
    maskd = np.zeros((128, 4 * 512), dtype=np.float32)
    for k in range(4):
        maskd[:, k * 512 + k * 128: k * 512 + (k + 1) * 128] = tri
    ones_j = np.ones((128, 128), dtype=np.float32)
    ones_row = np.ones((1, 128), dtype=np.float32)
    return dict(c2=c2, s2=s2, perm=perm, ident=ident, maskd=maskd,
                ones_j=ones_j, ones_row=ones_row)


def build_program(dev_stop=None, dev_dumps=(), p1_on=True):
    nc = bass.Bass("TRN2", target_bir_lowering=False)
    C = make_consts()

    xb = nc.dram_tensor("xb", [T, D], F32, kind="ExternalInput").ap()
    w_q = nc.dram_tensor("Wq", [D, D], F32, kind="ExternalInput").ap()
    w_k = nc.dram_tensor("Wk", [D, KV], F32, kind="ExternalInput").ap()
    w_v = nc.dram_tensor("Wv", [D, KV], F32, kind="ExternalInput").ap()
    w_p = nc.dram_tensor("Wproj", [D, D], F32, kind="ExternalInput").ap()
    gain = nc.dram_tensor("q_gain", [1, H], F32, kind="ExternalInput").ap()
    a_q = nc.dram_tensor("q_A", [D, RANK], F32, kind="ExternalInput").ap()
    a_k = nc.dram_tensor("k_A", [D, RANK], F32, kind="ExternalInput").ap()
    a_v = nc.dram_tensor("v_A", [D, RANK], F32, kind="ExternalInput").ap()
    a_p = nc.dram_tensor("p_A", [D, RANK], F32, kind="ExternalInput").ap()
    b_q = nc.dram_tensor("q_B", [RANK, D], F32, kind="ExternalInput").ap()
    b_k = nc.dram_tensor("k_B", [RANK, KV], F32, kind="ExternalInput").ap()
    b_v = nc.dram_tensor("v_B", [RANK, KV], F32, kind="ExternalInput").ap()
    b_p = nc.dram_tensor("p_B", [RANK, D], F32, kind="ExternalInput").ap()
    w_up = nc.dram_tensor("W_up", [DE, D], F32, kind="ExternalInput").ap()
    w_gate = nc.dram_tensor("W_gate", [DE, D], F32, kind="ExternalInput").ap()
    w_down = nc.dram_tensor("W_down", [D, DE], F32, kind="ExternalInput").ap()

    out = nc.dram_tensor("out", [T, D], F32, kind="ExternalOutput").ap()

    y_dram = nc.dram_tensor("y_scr", [D, T], BF16, kind="Internal").ap()
    xmid_dram = nc.dram_tensor("xmid_scr", [T, D], F32, kind="Internal").ap()
    hid_dram = nc.dram_tensor("hid_scr", [DE, T], BF16, kind="Internal").ap()
    tdT_dram = nc.dram_tensor("tdT_scr", [4, 128, 64, 512], BF16,
                              kind="Internal").ap()

    c2_d = nc.inline_tensor(C["c2"], "c2c").ap()
    s2_d = nc.inline_tensor(C["s2"], "s2c").ap()
    ident_d = nc.inline_tensor(C["ident"], "identc").ap()
    maskd_d = nc.inline_tensor(C["maskd"], "maskd").ap()
    onesj_d = nc.inline_tensor(C["ones_j"], "onesj").ap()
    onesr_d = nc.inline_tensor(C["ones_row"], "onesr").ap()

    dev = {}
    if DEV:
        for name, shape, dt in dev_dumps:
            dev[name] = nc.dram_tensor(name, shape, dt, kind="ExternalOutput").ap()

    with ExitStack() as outer:
        ctx = Ctx(nc)
        for s in ("w", "g", "t", "v", "s", "wa", "wb", "wc", "ws",
                  "ga", "gb", "gy", "gy2", "wy", "wy2", "wx", "wx2", "gm", "gh",
                  "gt2", "go", "go2", "wh", "u0", "u1", "u2", "u3"):
            ctx.new_sem(s, outer)

        slabA_h = outer.enter_context(nc.sbuf_tensor("slabA", [128, ND * T], BF16))
        slabA = slabA_h[:].rearrange("p (dt t) -> p dt t", dt=ND)
        w1 = outer.enter_context(nc.sbuf_tensor("w1", [128, 2048], F32))
        w2 = outer.enter_context(nc.sbuf_tensor("w2", [128, 2048], F32))
        b1 = outer.enter_context(nc.sbuf_tensor("b1", [128, 2048], BF16))
        b2 = outer.enter_context(nc.sbuf_tensor("b2", [128, 2048], BF16))
        st = outer.enter_context(nc.sbuf_tensor("stt", [128, 104], F32))
        id_bf = outer.enter_context(nc.sbuf_tensor("id_bf", [128, 128], BF16))
        id_f = outer.enter_context(nc.sbuf_tensor("id_f", [128, 128], F32))
        prow = outer.enter_context(nc.sbuf_tensor("prow", [1, 128], F32))
        zrow = outer.enter_context(nc.sbuf_tensor("zrow", [1, 128], F32))
        srow = outer.enter_context(nc.sbuf_tensor("srow", [1, 128], F32))
        ptmp = outer.enter_context(nc.sbuf_tensor("ptmp", [128, 32], F32))
        mask_bf = outer.enter_context(nc.sbuf_tensor("mask_bf", [128, 2048], BF16))
        jj_bf = outer.enter_context(nc.sbuf_tensor("jj_bf", [128, 128], BF16))
        onesr_f = outer.enter_context(nc.sbuf_tensor("onesr_f", [1, 128], F32))
        epsc = outer.enter_context(nc.sbuf_tensor("epsc", [128, 4], F32))
        pacc = outer.enter_context(nc.sbuf_tensor("pacc", [128, 768], F32))
        jj_f = outer.enter_context(nc.sbuf_tensor("jj_f", [128, 128], F32))
        scl = outer.enter_context(nc.sbuf_tensor("scl", [128, 64], F32))
        devcap = outer.enter_context(nc.sbuf_tensor("devcap", [128, 104], F32)) if DEV else None

        ctx.dma("gpsimd", id_bf[:], ident_d)
        ctx.dma("gpsimd", id_f[:], ident_d)
        ctx.dma("gpsimd", mask_bf[:], maskd_d)
        ctx.dma("gpsimd", jj_bf[:], onesj_d)
        ctx.dma("gpsimd", jj_f[:], onesj_d)
        ctx.dma("gpsimd", onesr_f[:], onesr_d)
        g_consts = ctx.cnt["g"]
        ctx.emit("vector", lambda e: e.memset(st[:], 0.0))
        ctx.emit("vector", lambda e: e.memset(epsc[:, 0:1], EPS_LN))
        ctx.emit("vector", lambda e: e.memset(epsc[:, 1:2], EPS_RMS))
        ctx.emit("vector", lambda e: e.memset(epsc[:, 2:3], 1e-24))
        ctx.emit("vector", lambda e: e.memset(zrow[:], 0.0))
        ctx.emit("vector", lambda e: e.memset(scl[:], 0.0), inc="v")
        ctx.wait("vector", "v")

        ctx.emit("vector", lambda e: e.memset(scl[:, 48:49], 1048576.0), inc="v")
        ctx.wait("vector", "v")
        for it in range(40):
            esrc = 48 if it % 2 == 0 else 52
            edst = 52 if it % 2 == 0 else 48
            ctx.emit("vector", lambda e, esrc=esrc: e.tensor_scalar(
                scl[:, 49:50], scl[:, esrc:esrc + 1], 1.0, None, ALU.add),
                inc="v")
            ctx.wait("vector", "v")
            ctx.emit("vector", lambda e, esrc=esrc: e.tensor_tensor(
                scl[:, 50:51], scl[:, 49:50], scl[:, esrc:esrc + 1],
                ALU.is_equal), inc="v")
            ctx.wait("vector", "v")
            ctx.emit("vector", lambda e: e.tensor_scalar(
                scl[:, 51:52], scl[:, 50:51], -1.0, 2.0, ALU.mult, ALU.add),
                inc="v")
            ctx.wait("vector", "v")
            ctx.emit("vector", lambda e, esrc=esrc, edst=edst: e.tensor_tensor(
                scl[:, edst:edst + 1], scl[:, esrc:esrc + 1], scl[:, 51:52],
                ALU.mult), inc="v")
            ctx.wait("vector", "v")
        ctx.emit("vector", lambda e: e.tensor_scalar(
            scl[:, 60:61], scl[:, 48:49], 2.0, None, ALU.mult), inc="v")
        ctx.wait("vector", "v")

        with ExitStack() as p1s:
            xt0 = p1s.enter_context(nc.sbuf_tensor("xt0", [128, 2048], F32))
            xt1 = p1s.enter_context(nc.sbuf_tensor("xt1", [128, 2048], F32))
            ptr = p1s.enter_context(nc.psum_tensor("ptr_ln", [128, 2048], BF16))
            xts = [xt0, xt1]
            dma_done = [0] * NT
            norm_done = [0] * NT
            tpose_done = [0] * NT
            evac_done = [0] * NT
            for tt in range(NT):
                xt = xts[tt % 2]
                wsem = "wa" if tt % 2 == 0 else "wb"
                if tt >= 2:
                    ctx.wait("sync", "s", norm_done[tt - 2])
                dma_done[tt] = ctx.dma("sync", xt[:], xb[tt * 128:(tt + 1) * 128, :],
                                       sem=wsem)
                ctx.wait("vector", wsem)
                ctx.emit("vector", lambda e, xt=xt: e.reduce_sum(
                    st[:, 0:1], xt[:], axis=AX.X), inc="v")
                ctx.wait("vector", "v")
                ctx.wait("scalar", wsem)
                if tt == 0:
                    ctx.wait("scalar", "v", ctx.cnt["v"])
                ctx.emit("scalar", lambda e, xt=xt: e.activation(
                    b2[:], xt[:], AF.Square, accum_out=st[:, 8:9]), inc="s")
                ctx.wait("vector", "s", ctx.cnt["s"])
                ctx.emit("vector", lambda e: e.tensor_scalar(
                    st[:, 16:17], st[:, 0:1], 1.0 / D, None, ALU.mult))
                ctx.emit("vector", lambda e: e.tensor_scalar(
                    st[:, 24:25], st[:, 8:9], 1.0 / D, None, ALU.mult), inc="v")
                ctx.wait("vector", "v")
                ctx.emit("vector", lambda e: e.scalar_tensor_tensor(
                    st[:, 32:33], st[:, 16:17], -1.0, st[:, 16:17], ALU.mult, ALU.mult),
                    inc="v")
                ctx.wait("vector", "v")
                ctx.emit("vector", lambda e: e.tensor_tensor(
                    st[:, 40:41], st[:, 24:25], st[:, 32:33], ALU.add), inc="v")
                ctx.wait("scalar", "v", ctx.cnt["v"])
                ctx.emit("scalar", lambda e: e.activation(
                    st[:, 48:49], st[:, 40:41], AF.Ln, bias=epsc[:, 0:1], scale=1.0),
                    inc="s")
                ctx.wait("scalar", "s")
                ctx.emit("scalar", lambda e: e.activation(
                    st[:, 56:57], st[:, 48:49], AF.Exp, bias=0.0, scale=-0.5), inc="s")
                ctx.wait("vector", "s", ctx.cnt["s"])
                ctx.emit("vector", lambda e: e.scalar_tensor_tensor(
                    st[:, 64:65], st[:, 16:17], -1.0, st[:, 56:57], ALU.mult, ALU.mult),
                    inc="v")
                if DEV and tt == 0:
                    ctx.wait("vector", "s", ctx.cnt["s"])
                    ctx.wait("vector", "v")
                    ctx.emit("vector", lambda e: e.tensor_copy(devcap[:], st[:]), inc="v")
                ctx.wait("scalar", "v", ctx.cnt["v"])
                if tt >= 1:
                    ctx.wait("scalar", "t", tpose_done[tt - 1])
                norm_done[tt] = ctx.emit("scalar", lambda e, xt=xt: e.activation(
                    b1[:], xt[:], AF.Identity, bias=st[:, 64:65], scale=st[:, 56:57]),
                    inc="s")
                if tt == 0:
                    ctx.wait("tensor", "g", g_consts)
                ctx.wait("tensor", "s", norm_done[tt])
                if tt >= 1:
                    ctx.wait("tensor", "v", evac_done[tt - 1])
                for grp in range(4):
                    half = (grp % 2) * 1024
                    if grp >= 2:
                        ctx.wait("tensor", "v", ctx.cnt["v"])
                    for j in range(4):
                        dt = grp * 4 + j
                        ctx.emit("tensor", lambda e, dt=dt, half=half, j=j:
                                 e.transpose(ptr[:, half + j * 128: half + (j + 1) * 128],
                                             b1[:, dt * 128:(dt + 1) * 128], id_bf[:]),
                                 inc=("t" if True else None))
                    ctx.wait("vector", "t", ctx.cnt["t"])
                    ctx.emit("vector", lambda e, grp=grp, half=half, tt=tt:
                             e.tensor_copy(
                                 slabA[:, grp * 4:(grp + 1) * 4, tt * 128:(tt + 1) * 128],
                                 ptr[:, half:half + 512].rearrange("p (j c) -> p j c", j=4)),
                             inc="v")
                tpose_done[tt] = ctx.cnt["t"]
                evac_done[tt] = ctx.cnt["v"]

        if DEV and "dev_xn" in dev:
            ctx.wait("gpsimd", "v", ctx.cnt["v"])
            ctx.wait("gpsimd", "s", ctx.cnt["s"])
            ctx.dma("gpsimd", dev["dev_xn"], slabA_h[:])
        if DEV and "dev_b1" in dev:
            ctx.wait("gpsimd", "s", ctx.cnt["s"])
            ctx.dma("gpsimd", dev["dev_b1"], b1[:])
        if DEV and "dev_st" in dev:
            ctx.wait("gpsimd", "s", ctx.cnt["s"])
            ctx.dma("gpsimd", dev["dev_st"], devcap[:])

        if dev_stop == "ln1":
            _final_waits(ctx)
            with nc.Block() as block:
                ctx.finalize(block)
            return nc

        ln1_done_v = ctx.cnt["v"]

        ctx.phase_barrier()
        with ExitStack() as p2s:
            sb = lambda n, s, d: p2s.enter_context(nc.sbuf_tensor(n, s, d))
            c2_s = sb("c2_s", [64, 2048], F32)
            s2_s = sb("s2_s", [64, 2048], F32)
            rb = sb("rb", [64, 2048], F32)
            rb2 = sb("rb2", [64, 2048], F32)
            wcol_bf = sb("wcol_bf", [128, 2048], BF16)
            k_bf = sb("k_bf", [128, 2048], BF16)
            q_bf = sb("q_bf", [128, 2048], BF16)
            vhat = sb("vhat", [128, 2048], F32)
            v_tok = sb("v_tok", [128, 2048], BF16)
            ybuf = sb("ybuf", [128, 1024], BF16)
            pt_bf = sb("pt_bf", [128, 16 * 512], BF16)
            rstdb = sb("rstdb", [128, 2048], F32)
            aq_s = sb("aq_s", [128, 256], BF16)
            ak_s = sb("ak_s", [128, 256], BF16)
            av_s = sb("av_s", [128, 256], BF16)
            ap_s = sb("ap_s", [128, 256], BF16)
            bmats_h = sb("bmats", [80, 2048], BF16)
            bq_s = bmats_h[0:16, :]
            bk_s = bmats_h[32:48, 0:512]
            bv_s = bmats_h[64:80, 0:512]
            lora1_h = sb("lora1", [80, 2048], BF16)
            qa_b = lora1_h[0:16, :]
            ka_b = lora1_h[32:48, :]
            va_b = lora1_h[64:80, :]
            gain_s = sb("gain_s", [1, 16], F32)
            gams = sb("gams", [128, 16], F32)
            pdump = sb("pdump", [128, 1024], F32)
            pdump1 = sb("pdump1", [128, 1024], F32)
            pw0 = sb("pw0", [128, 1024], F32)
            pw1 = sb("pw1", [128, 1024], F32)
            a_ps = p2s.enter_context(nc.psum_tensor("a_ps", [128, 2048], F32))
            pt_ps = p2s.enter_context(nc.psum_tensor("pt_ps", [128, 2048], BF16))
            y_ps = p2s.enter_context(nc.psum_tensor("y_ps", [128, 512], F32))
            cb_ps = p2s.enter_context(nc.psum_tensor("cb_ps", [128, 512], F32))

            for j in range(16):
                sc = (j % 4) * 128
                if sc:
                    ctx.emit("vector", lambda e, j=j, sc=sc: e.memset(
                        pt_bf[:, j * 512: j * 512 + sc], 0.0), inc="v")
            ctx.wait("vector", "v")

            ctx.dma("gpsimd", c2_s[:], c2_d)
            ctx.dma("gpsimd", s2_s[:], s2_d)
            for dst, src in ((aq_s, a_q), (ak_s, a_k), (av_s, a_v), (ap_s, a_p)):
                ctx.dma("gpsimd", dst[:].rearrange("p (dt r) -> p dt r", dt=ND),
                        src.rearrange("(dt p) r -> p dt r", p=128))
            ctx.dma("gpsimd", bq_s, b_q)
            ctx.dma("gpsimd", bk_s, b_k)
            ctx.dma("gpsimd", bv_s, b_v)
            ctx.dma("gpsimd", gain_s[:], gain)
            g_setup = ctx.cnt["g"]
            ctx.wait("tensor", "g", g_setup)
            ctx.emit("tensor", lambda e: e.matmul(
                cb_ps[:, 0:16], onesr_f[:], gain_s[:], start=True, stop=True),
                inc="t")
            ctx.wait("vector", "t", ctx.cnt["t"])
            ctx.emit("vector", lambda e: e.tensor_scalar(
                gams[:], cb_ps[:, 0:16], SCALE_QK, None, ALU.mult), inc="v")
            gams_done = ctx.cnt["v"]

            pass1_srcs = []
            for mi, wmat in enumerate((w_down, w_up, w_gate)):
                flat = wmat.rearrange("r c -> (r c)").rearrange(
                    "(b c x) -> c b x", b=128, x=1024)
                for ch in range(128):
                    pass1_srcs.append((mi, ch, flat[ch]))
            n_p1 = len(pass1_srcs)
            assert n_p1 == 384, n_p1
            p1_dma = [0] * n_p1
            p1_use = [0] * n_p1

            def emit_pass1(i):
                mi, ch, src = pass1_srcs[i]
                buf = pw0 if i % 2 == 0 else pw1
                dmp = pdump if i % 2 == 0 else pdump1
                psem = "ga" if i % 2 == 0 else "gb"
                if i >= 2:
                    ctx.wait("gpsimd", "s", p1_use[i - 2])
                p1_dma[i] = ctx.dma("gpsimd", buf[:], src, sem=psem)
                ctx.wait("scalar", psem)
                p1_use[i] = ctx.emit("scalar", lambda e, buf=buf, dmp=dmp, i=i:
                                     e.activation(
                                         dmp[:], buf[:], AF.Abs,
                                         accum_out=pacc[:, i:i + 1]), inc="s")

            ctx.wait("tensor", "v", gams_done)
            ctx.wait("tensor", "v", ln1_done_v)
            for amat, outb in ((aq_s, qa_b), (ak_s, ka_b), (av_s, va_b)):
                for tc in range(4):
                    ctx.wait("tensor", "s", ctx.cnt["s"])
                    for dt in range(ND):
                        ctx.emit("tensor", lambda e, amat=amat, dt=dt, tc=tc:
                                 e.matmul(cb_ps[0:16, 0:512],
                                          amat[:, dt * 16:(dt + 1) * 16],
                                          slabA[:, dt, tc * 512:(tc + 1) * 512],
                                          start=(dt == 0), stop=(dt == ND - 1)),
                                 inc=("t" if dt == ND - 1 else None))
                    ctx.wait("scalar", "t", ctx.cnt["t"])
                    ctx.emit("scalar", lambda e, outb=outb, tc=tc: e.copy(
                        outb[0:16, tc * 512:(tc + 1) * 512], cb_ps[0:16, 0:512]),
                        inc="s")
            lora1_done_s = ctx.cnt["s"]
            if DEV and dev_stop == "k0":
                if "dev_ka_early" in dev:
                    ctx.wait("gpsimd", "s", ctx.cnt["s"])
                    ctx.dma("gpsimd", dev["dev_ka_early"], ka_b)
            if DEV and dev_stop == "lora0":
                ctx.wait("gpsimd", "s", ctx.cnt["s"])
                ctx.wait("gpsimd", "v", ctx.cnt["v"])
                for nm, src in (("dev_qa", qa_b), ("dev_ka", ka_b),
                                ("dev_va", va_b), ("dev_xn", slabA_h[:])):
                    if nm in dev:
                        ctx.dma("gpsimd", dev[nm], src)
                _final_waits(ctx)
                with nc.Block() as block:
                    ctx.finalize(block)
                return nc

            p1_next = 0
            if not p1_on:
                def emit_pass1(i):
                    pass
            while p1_next < 4:
                emit_pass1(p1_next); p1_next += 1

            def build_row(wmat, col0, bmat, bcol0, stage1, is_q, hq, dst_bf,
                          want_vhat=False):
                ctx.wait("gpsimd", "t", wcol_guard[0])
                for dt in range(ND):
                    ctx.dma("gpsimd", wcol_bf[:, dt * 128:(dt + 1) * 128],
                            wmat[dt * 128:(dt + 1) * 128, col0:col0 + 128],
                            sem="wc")
                ctx.wait("tensor", "wc")
                ctx.wait("tensor", "s", ctx.cnt["s"])
                ctx.wait("tensor", "v", ctx.cnt["v"])
                for pair in range(2):
                    if pair == 1:
                        ctx.wait("tensor", "s", ctx.cnt["s"])
                    for dt in range(ND):
                        for sub in range(2):
                            tc = pair * 2 + sub
                            ctx.emit("tensor",
                                     lambda e, dt=dt, sub=sub, tc=tc: e.matmul(
                                         a_ps[:, sub * 512:(sub + 1) * 512],
                                         wcol_bf[:, dt * 128:(dt + 1) * 128],
                                         slabA[:, dt, tc * 512:(tc + 1) * 512],
                                         start=(dt == 0), stop=False))
                    for sub in range(2):
                        tc = pair * 2 + sub
                        if SKIP_LORA_B:
                            ctx.emit("tensor", lambda e, sub=sub: e.matmul(
                                a_ps[:, sub * 512:(sub + 1) * 512],
                                wcol_bf[:, 0:128],
                                slabA[:, 0, 0:512],
                                start=False, stop=True),
                                inc=("t" if sub == 1 else None))
                            continue
                        ctx.emit("tensor", lambda e, sub=sub, tc=tc, bmat=bmat,
                                 bcol0=bcol0, stage1=stage1: e.matmul(
                                     a_ps[:, sub * 512:(sub + 1) * 512],
                                     bmat[0:16, bcol0:bcol0 + 128],
                                     stage1[0:16, tc * 512:(tc + 1) * 512],
                                     start=False, stop=True),
                                 inc=("t" if sub == 1 else None))
                    ctx.wait("scalar", "t", ctx.cnt["t"])
                    ctx.emit("scalar", lambda e, pair=pair: e.copy(
                        w1[:, pair * 1024:(pair + 1) * 1024],
                        a_ps[:, 0:1024]), inc="s")
                evac_done = ctx.cnt["s"]
                wcol_guard[0] = ctx.cnt["t"]
                if TRUNC <= 1:
                    return ctx.cnt["v"]
                ctx.wait("scalar", "s")
                ctx.emit("scalar", lambda e: e.activation(
                    b2[:], w1[:], AF.Square), inc="s")
                sq_done = ctx.cnt["s"]
                ctx.wait("scalar", "v", ctx.cnt["v"])
                for c in range(4):
                    ctx.wait("tensor", "s", sq_done)
                    if c > 0:
                        ctx.wait("tensor", "s", ctx.cnt["s"])
                    ctx.emit("tensor", lambda e, c=c: e.matmul(
                        a_ps[:, 1024:1536], jj_bf[:],
                        b2[:, c * 512:(c + 1) * 512], start=True, stop=True),
                        inc="t")
                    ctx.wait("scalar", "t", ctx.cnt["t"])
                    if want_vhat:
                        ctx.emit("scalar", lambda e, c=c: e.activation(
                            w2[:, c * 512:(c + 1) * 512], a_ps[:, 1024:1536],
                            AF.Ln, bias=epsc[:, 2:3], scale=1.0), inc="s")
                    else:
                        ctx.emit("scalar", lambda e, c=c: e.activation(
                            w2[:, c * 512:(c + 1) * 512], a_ps[:, 1024:1536],
                            AF.Ln, bias=epsc[:, 1:2], scale=1.0 / HD), inc="s")
                    ctx.wait("scalar", "s")
                    ctx.emit("scalar", lambda e, c=c: e.activation(
                        rstdb[:, c * 512:(c + 1) * 512], w2[:, c * 512:(c + 1) * 512],
                        AF.Exp, bias=0.0, scale=-0.5), inc="s")
                rstd_done = ctx.cnt["s"]
                if TRUNC <= 2:
                    return ctx.cnt["v"]

                if want_vhat:
                    ctx.wait("vector", "s", rstd_done)
                    ctx.emit("vector", lambda e: e.tensor_tensor(
                        vhat[:], w1[:], rstdb[:], ALU.mult), inc="v")
                    ctx.emit("vector", lambda e: e.tensor_copy(b1[:], w1[:]),
                             inc="v")
                    vb_done = ctx.cnt["v"]
                    ctx.wait("tensor", "v", vb_done)
                    for grp in range(4):
                        half = (grp % 2) * 1024
                        if grp >= 2:
                            ctx.wait("tensor", "v", ctx.cnt["v"])
                        for j in range(4):
                            jt = grp * 4 + j
                            ctx.emit("tensor", lambda e, jt=jt, half=half, j=j:
                                     e.transpose(
                                         pt_ps[:, half + j * 128: half + (j + 1) * 128],
                                         b1[:, jt * 128:(jt + 1) * 128], id_bf[:]),
                                     inc="t")
                        ctx.wait("vector", "t", ctx.cnt["t"])
                        ctx.emit("vector", lambda e, grp=grp, half=half:
                                 e.tensor_copy(
                                     v_tok[:, grp * 512:(grp + 1) * 512],
                                     pt_ps[:, half:half + 512]), inc="v")
                    return ctx.cnt["v"]

                ctx.wait("sync", "s", rstd_done)
                ctx.dma("sync", w2[0:32, :], w1[32:64, :])
                swap_w = ctx.dma("sync", w2[32:64, :], w1[0:32, :])
                if TRUNC <= 3:
                    return ctx.cnt["v"]
                ctx.wait("vector", "s", rstd_done)
                ctx.wait("vector", "w", swap_w)
                ctx.emit("vector", lambda e: e.tensor_tensor(
                    rb[:], w1[0:64, :], c2_s[:], ALU.mult))
                ctx.emit("vector", lambda e: e.tensor_tensor(
                    rb2[:], w2[0:64, :], s2_s[:], ALU.mult), inc="v")
                ctx.wait("vector", "v")
                ctx.emit("vector", lambda e: e.tensor_tensor(
                    w1[0:64, :], rb[:], rb2[:], ALU.add), inc="v")
                ctx.wait("vector", "v")
                if is_q:
                    ctx.wait("vector", "v", gams_done)
                    ctx.emit("vector", lambda e, hq=hq: e.scalar_tensor_tensor(
                        dst_bf[:], w1[:], gams[:, hq:hq + 1], rstdb[:],
                        ALU.mult, ALU.mult), inc="v")
                else:
                    ctx.emit("vector", lambda e: e.tensor_tensor(
                        dst_bf[:], w1[:], rstdb[:], ALU.mult), inc="v")
                return ctx.cnt["v"]

            y_dma_done = [0]
            wcol_guard = [0]
            exp_ch_s = [0, 0, 0, 0]
            for h in range(HKV):
                k_done = build_row(w_k, h * 128, bk_s, h * 128, ka_b, False,
                                   None, k_bf)
                if DEV and dev_stop == "k0" and h == 0:
                    ctx.wait("gpsimd", "v", ctx.cnt["v"])
                    ctx.wait("gpsimd", "s", ctx.cnt["s"])
                    for nm, src in (("dev_w1", w1[:]), ("dev_rstdb", rstdb[:]),
                                    ("dev_k", k_bf[:]), ("dev_wcol", wcol_bf[:]),
                                    ("dev_ka", ka_b[:]), ("dev_sq", b2[:]),
                                    ("dev_w2", w2[:])):
                        if nm in dev:
                            ctx.dma("gpsimd", dev[nm], src)
                    _final_waits(ctx)
                    with nc.Block() as block:
                        ctx.finalize(block)
                    return nc
                v_done = build_row(w_v, h * 128, bv_s, h * 128, va_b, False,
                                   None, None, want_vhat=True)
                for g in range(4):
                    hq = h * 4 + g
                    q_done = build_row(w_q, hq * 128, bq_s, hq * 128, qa_b,
                                       True, hq, q_bf)
                    ctx.wait("vector", "gy")
                    ctx.wait("vector", "gy2")
                    pt_parity = [0]
                    pend_pt = [None]
                    pt_last_t = [0]

                    def emit_pt(piq, pbf_done):
                        ctx.wait("tensor", "v", pbf_done)
                        col = piq % 4
                        for jj in range((piq + 1 + 7) // 8):
                            cnt8 = min(8, piq + 1 - jj * 8)
                            par = pt_parity[0] % 2
                            pt_parity[0] += 1
                            if pt_parity[0] > 2:
                                ctx.wait("tensor", "v", ctx.cnt["v"])
                            for u in range(cnt8):
                                jt = jj * 8 + u
                                ctx.emit("tensor",
                                         lambda e, jt=jt, par=par, u=u:
                                         e.transpose(
                                             pt_ps[:, par * 1024 + u * 128:
                                                   par * 1024 + (u + 1) * 128],
                                             b1[:, jt * 128:(jt + 1) * 128],
                                             id_bf[:]), inc="t")
                            ctx.wait("vector", "t", ctx.cnt["t"])
                            ctx.emit("vector",
                                     lambda e, jj=jj, cnt8=cnt8, par=par,
                                     col=col: e.tensor_copy(
                                         pt_bf[:].rearrange(
                                             "p (j c) -> p j c", j=16)
                                         [:, jj * 8:jj * 8 + cnt8,
                                          col * 128:(col + 1) * 128],
                                         pt_ps[:, par * 1024:
                                               par * 1024 + cnt8 * 128]
                                         .rearrange("p (j c) -> p j c",
                                                    j=cnt8)), inc="v")
                        pt_last_t[0] = ctx.cnt["t"]

                    for G in range(4):
                        for iq in range(4 * G, 4 * G + 4):
                            W = (iq + 1) * 128
                            nch = (W + 511) // 512
                            ctx.wait("tensor", "v", q_done)
                            for c in range(nch):
                                n = min(512, W - c * 512)
                                last = (c == nch - 1)
                                ctx.wait("tensor", "s", exp_ch_s[c])
                                ctx.emit("tensor",
                                         lambda e, c=c, n=n, last=last, iq=iq:
                                         e.matmul(a_ps[:, c * 512:c * 512 + n],
                                                  q_bf[:, iq * 128:(iq + 1) * 128],
                                                  k_bf[:, c * 512:c * 512 + n],
                                                  start=True, stop=not last),
                                         inc=(None if last else "t"))
                                if not last:
                                    ctx.wait("vector", "t", ctx.cnt["t"])
                                    ctx.emit("vector", lambda e, c=c, n=n:
                                             e.reduce_max(
                                                 st[:, 80 + c:81 + c],
                                                 a_ps[:, c * 512:c * 512 + n],
                                                 axis=AX.X, negate=True),
                                             inc="v")
                            kd = iq % 4
                            cd = iq // 4
                            nd_ = W - cd * 512
                            ctx.emit("tensor", lambda e, kd=kd, cd=cd, nd_=nd_:
                                     e.matmul(
                                         a_ps[:, cd * 512:cd * 512 + nd_],
                                         id_bf[:],
                                         mask_bf[:, kd * 512:kd * 512 + nd_],
                                         start=False, stop=True), inc="t")
                            s_done = ctx.cnt["t"]
                            ctx.wait("vector", "t", s_done)
                            ctx.emit("vector", lambda e, cd=cd, nd_=nd_:
                                     e.reduce_max(
                                         st[:, 80 + cd:81 + cd],
                                         a_ps[:, cd * 512:cd * 512 + nd_],
                                         axis=AX.X, negate=True), inc="v")
                            ctx.wait("vector", "v")
                            if nch > 1:
                                ctx.emit("vector", lambda e, nch=nch:
                                         e.tensor_reduce(
                                             st[:, 86:87], st[:, 80:80 + nch],
                                             axis=AX.X, op=ALU.min), inc="v")
                                ctx.wait("vector", "v")
                                bias_col = st[:, 86:87]
                            else:
                                bias_col = st[:, 80:81]
                            ctx.wait("scalar", "v", ctx.cnt["v"])
                            nech = (W + 1023) // 1024
                            for c in range(nech):
                                n = min(1024, W - c * 1024)
                                ctx.emit("scalar", lambda e, c=c, n=n,
                                         bias_col=bias_col: e.activation(
                                             w2[:, c * 1024:c * 1024 + n],
                                             a_ps[:, c * 1024:c * 1024 + n],
                                             AF.Exp, bias=bias_col, scale=1.0,
                                             accum_out=st[:, 88 + c:89 + c]),
                                         inc="s")
                                exp_ch_s[2 * c] = ctx.cnt["s"]
                                if 2 * c + 1 < 4:
                                    exp_ch_s[2 * c + 1] = ctx.cnt["s"]
                            exp_done = ctx.cnt["s"]
                            if pend_pt[0] is not None:
                                emit_pt(*pend_pt[0])
                                pend_pt[0] = None
                            stop_at = 4 + ((hq * 16 + iq + 1) * 3) // 2
                            while p1_next < stop_at and p1_next < n_p1:
                                emit_pass1(p1_next); p1_next += 1
                            ctx.wait("vector", "s", exp_done)
                            if nech > 1:
                                ctx.emit("vector", lambda e, nech=nech:
                                         e.reduce_sum(
                                             st[:, 94:95], st[:, 88:88 + nech],
                                             axis=AX.X), inc="v")
                                ctx.wait("vector", "v")
                                den_col = st[:, 94:95]
                            else:
                                den_col = st[:, 88:89]
                            ctx.emit("vector", lambda e, den_col=den_col:
                                     e.reciprocal(st[:, 96:97], den_col),
                                     inc="v")
                            ctx.wait("vector", "v")
                            ctx.wait("vector", "t", pt_last_t[0])
                            ctx.emit("vector", lambda e, W=W: e.tensor_scalar(
                                b1[:, 0:W], w2[:, 0:W], st[:, 96:97], None,
                                ALU.mult), inc="v")
                            pend_pt[0] = (iq, ctx.cnt["v"])
                        if pend_pt[0] is not None:
                            emit_pt(*pend_pt[0])
                            pend_pt[0] = None
                        for j in range(4 * G + 1, 4 * G + 4):
                            sc = (j - 4 * G) * 128
                            ctx.emit("vector", lambda e, j=j, sc=sc: e.memset(
                                pt_bf[:, j * 512: j * 512 + sc], 0.0),
                                inc=("v" if j == 4 * G + 3 else None))
                        ctx.wait("vector", "v")
                        ctx.wait("tensor", "v", ctx.cnt["v"])
                        for j in range(4 * G + 4):
                            ctx.emit("tensor", lambda e, j=j, G=G: e.matmul(
                                y_ps[:],
                                v_tok[:, j * 128:(j + 1) * 128],
                                pt_bf[:, j * 512: (j + 1) * 512],
                                start=(j == 0), stop=(j == 4 * G + 3)),
                                inc=("t" if j == 4 * G + 3 else None))
                        av_done = ctx.cnt["t"]
                        ctx.wait("vector", "t", av_done)
                        ctx.emit("vector", lambda e, G=G: e.tensor_tensor(
                            b2[:, 0:512], y_ps[:],
                            vhat[:, G * 512:(G + 1) * 512], ALU.mult), inc="v")
                        ctx.wait("tensor", "v", ctx.cnt["v"])
                        ctx.emit("tensor", lambda e: e.matmul(
                            cb_ps[:], jj_bf[:], b2[:, 0:512],
                            start=True, stop=True), inc="t")
                        ctx.wait("vector", "t", ctx.cnt["t"])
                        ctx.emit("vector", lambda e, G=G: e.tensor_tensor(
                            w1[:, 0:512], cb_ps[:],
                            vhat[:, G * 512:(G + 1) * 512], ALU.mult), inc="v")
                        ctx.wait("vector", "v")
                        par = G % 2
                        gsem = ("gy", "gy2")[par]
                        if G >= 2:
                            ctx.wait("vector", gsem)
                        ctx.emit("vector", lambda e, par=par: e.tensor_tensor(
                            ybuf[:, par * 512:(par + 1) * 512], y_ps[:],
                            w1[:, 0:512], ALU.subtract), inc="v")
                        ctx.wait("gpsimd", "v", ctx.cnt["v"])
                        y_dma_done[0] = ctx.dma(
                            "gpsimd",
                            y_dram[hq * 128:(hq + 1) * 128, G * 512:(G + 1) * 512],
                            ybuf[:, par * 512:(par + 1) * 512], sem=gsem)
            while p1_next < n_p1:
                emit_pass1(p1_next); p1_next += 1
            y_all_g = ctx.cnt["g"]
            pass1_done_s = ctx.cnt["s"]

            if DEV:
                if "dev_k" in dev:
                    ctx.wait("gpsimd", "v", ctx.cnt["v"])
                    ctx.dma("gpsimd", dev["dev_k"], k_bf[:])
                if "dev_q" in dev:
                    ctx.wait("gpsimd", "v", ctx.cnt["v"])
                    ctx.dma("gpsimd", dev["dev_q"], q_bf[:])
                if "dev_vhat" in dev:
                    ctx.wait("gpsimd", "v", ctx.cnt["v"])
                    ctx.dma("gpsimd", dev["dev_vhat"], vhat[:])
                if "dev_vtok" in dev:
                    ctx.wait("gpsimd", "v", ctx.cnt["v"])
                    ctx.dma("gpsimd", dev["dev_vtok"], v_tok[:])
                if "dev_y" in dev:
                    ctx.wait("gpsimd", "g", y_all_g)
                    ctx.dma("gpsimd", dev["dev_y"], y_dram)

            if dev_stop in ("kv", "attn"):
                _final_waits(ctx)
                with nc.Block() as block:
                    ctx.finalize(block)
                return nc

        ctx.phase_barrier()
        with ExitStack() as late:
            sb3 = lambda n, s, d: late.enter_context(nc.sbuf_tensor(n, s, d))
            slabB_h = sb3("slabB", [128, ND * T], BF16)
            slabB = slabB_h[:].rearrange("p (dt t) -> p dt t", dt=ND)
            wproj_v = slabA

            with ExitStack() as p3s:
                sb = lambda n, s, d: p3s.enter_context(nc.sbuf_tensor(n, s, d))
                ycol0 = sb("ycol0", [128, 2048], BF16)
                ycol1 = sb("ycol1", [128, 2048], BF16)
                xt20 = sb("xt20", [128, 2048], F32)
                xt21 = sb("xt21", [128, 2048], F32)
                ap2_s = sb("ap2_s", [128, 256], BF16)
                pa_bf = sb("pa_bf", [16, 2048], BF16)
                pb_s = sb("pb_s", [16, 2048], BF16)
                pr_ps = p3s.enter_context(nc.psum_tensor("pr_ps", [128, 2048], F32))
                pp_ps = p3s.enter_context(nc.psum_tensor("pp_ps", [128, 512], F32))
                ptr2 = p3s.enter_context(nc.psum_tensor("ptr2", [128, 2048], BF16))

                for dt in range(ND):
                    ctx.dma("gpsimd", slabA[:, dt, :],
                            w_p[dt * 128:(dt + 1) * 128, :])
                ctx.dma("gpsimd", ap2_s[:].rearrange("p (dt r) -> p dt r", dt=ND),
                        a_p.rearrange("(dt p) r -> p dt r", p=128))
                ctx.dma("gpsimd", pb_s[:], b_p)
                g_p3 = ctx.cnt["g"]

                evac_v = [0, 0]
                mm_t = [0, 0]
                xm_dma = [0]
                tpose2 = [0]
                for tt in range(NT):
                    par = tt % 2
                    ycol = (ycol0, ycol1)[par]
                    xt2 = (xt20, xt21)[par]
                    ysem = ("wy", "wy2")[par]
                    xsem = ("wx", "wx2")[par]
                    if tt >= 2:
                        ctx.wait("sync", "t", mm_t[par])
                    ctx.dma("sync", ycol[:].rearrange("p (dt c) -> p dt c", dt=ND),
                            y_dram[:, tt * 128:(tt + 1) * 128]
                            .rearrange("(dt p) c -> p dt c", p=128), sem=ysem)
                    if tt >= 2:
                        ctx.wait("sync", "v", evac_v[par])
                    ctx.dma("sync", xt2[:], xb[tt * 128:(tt + 1) * 128, :], sem=xsem)
                    ctx.wait("tensor", ysem)
                    if tt == 0:
                        ctx.wait("tensor", "g", g_p3)
                    if tt >= 1:
                        ctx.wait("tensor", "s", ctx.cnt["s"])
                    for dt in range(ND):
                        ctx.emit("tensor", lambda e, dt=dt, ycol=ycol: e.matmul(
                            pp_ps[0:16, 0:128], ap2_s[:, dt * 16:(dt + 1) * 16],
                            ycol[:, dt * 128:(dt + 1) * 128],
                            start=(dt == 0), stop=(dt == ND - 1)),
                            inc=("t" if dt == ND - 1 else None))
                    ctx.wait("scalar", "t", ctx.cnt["t"])
                    ctx.emit("scalar", lambda e, tt=tt: e.copy(
                        pa_bf[0:16, tt * 128:(tt + 1) * 128], pp_ps[0:16, 0:128]),
                        inc="s")
                    pa_done = ctx.cnt["s"]
                    if tt >= 1:
                        ctx.wait("tensor", "v", evac_v[1 - par])
                    for dq in range(4):
                        for dt in range(ND):
                            ctx.emit("tensor",
                                     lambda e, dq=dq, dt=dt, ycol=ycol: e.matmul(
                                         pr_ps[:, dq * 512:(dq + 1) * 512],
                                         ycol[:, dt * 128:(dt + 1) * 128],
                                         wproj_v[:, dt, dq * 512:(dq + 1) * 512],
                                         start=(dt == 0), stop=False))
                        if dq == 0:
                            ctx.wait("tensor", "s", pa_done)
                        ctx.emit("tensor", lambda e, dq=dq, tt=tt: e.matmul(
                            pr_ps[:, dq * 512:(dq + 1) * 512],
                            pa_bf[0:16, tt * 128:(tt + 1) * 128],
                            pb_s[0:16, dq * 512:(dq + 1) * 512],
                            start=False, stop=True),
                            inc=("t" if dq == 3 else None))
                    mm_t[par] = ctx.cnt["t"]
                    ctx.wait("vector", "t", mm_t[par])
                    ctx.wait("vector", xsem)
                    if tt >= 1:
                        ctx.wait("vector", "gm")
                    for dq in range(4):
                        ctx.emit("vector", lambda e, dq=dq, xt2=xt2: e.tensor_tensor(
                            w1[:, dq * 512:(dq + 1) * 512],
                            pr_ps[:, dq * 512:(dq + 1) * 512],
                            xt2[:, dq * 512:(dq + 1) * 512], ALU.add),
                            inc=("v" if dq == 3 else None))
                    evac_v[par] = ctx.cnt["v"]
                    ctx.wait("gpsimd", "v", evac_v[par])
                    xm_dma[0] = ctx.dma("gpsimd", xmid_dram[tt * 128:(tt + 1) * 128, :],
                                        w1[:], sem="gm")
                    ctx.wait("vector", "v")
                    ctx.emit("vector", lambda e: e.reduce_sum(
                        st[:, 0:1], w1[:], axis=AX.X), inc="v")
                    ctx.wait("vector", "v")
                    ctx.wait("scalar", "v", ctx.cnt["v"])
                    ctx.emit("scalar", lambda e: e.activation(
                        b2[:], w1[:], AF.Square, accum_out=st[:, 8:9]), inc="s")
                    ctx.wait("vector", "s", ctx.cnt["s"])
                    ctx.emit("vector", lambda e: e.tensor_scalar(
                        st[:, 16:17], st[:, 0:1], 1.0 / D, None, ALU.mult))
                    ctx.emit("vector", lambda e: e.tensor_scalar(
                        st[:, 24:25], st[:, 8:9], 1.0 / D, None, ALU.mult), inc="v")
                    ctx.wait("vector", "v")
                    ctx.emit("vector", lambda e: e.scalar_tensor_tensor(
                        st[:, 32:33], st[:, 16:17], -1.0, st[:, 16:17],
                        ALU.mult, ALU.mult), inc="v")
                    ctx.wait("vector", "v")
                    ctx.emit("vector", lambda e: e.tensor_tensor(
                        st[:, 40:41], st[:, 24:25], st[:, 32:33], ALU.add), inc="v")
                    ctx.wait("scalar", "v", ctx.cnt["v"])
                    ctx.emit("scalar", lambda e: e.activation(
                        st[:, 48:49], st[:, 40:41], AF.Ln, bias=epsc[:, 0:1],
                        scale=1.0), inc="s")
                    ctx.wait("scalar", "s")
                    ctx.emit("scalar", lambda e: e.activation(
                        st[:, 56:57], st[:, 48:49], AF.Exp, bias=0.0, scale=-0.5),
                        inc="s")
                    ctx.wait("vector", "s", ctx.cnt["s"])
                    ctx.emit("vector", lambda e: e.scalar_tensor_tensor(
                        st[:, 64:65], st[:, 16:17], -1.0, st[:, 56:57],
                        ALU.mult, ALU.mult), inc="v")
                    ctx.wait("scalar", "v", ctx.cnt["v"])
                    if tt >= 1:
                        ctx.wait("scalar", "t", tpose2[0])
                    ctx.emit("scalar", lambda e: e.activation(
                        b1[:], w1[:], AF.Identity, bias=st[:, 64:65],
                        scale=st[:, 56:57]), inc="s")
                    norm2 = ctx.cnt["s"]
                    ctx.wait("tensor", "s", norm2)
                    for grp in range(4):
                        half = (grp % 2) * 1024
                        if grp >= 2:
                            ctx.wait("tensor", "v", ctx.cnt["v"])
                        for j in range(4):
                            dt = grp * 4 + j
                            ctx.emit("tensor", lambda e, dt=dt, half=half, j=j:
                                     e.transpose(
                                         ptr2[:, half + j * 128: half + (j + 1) * 128],
                                         b1[:, dt * 128:(dt + 1) * 128], id_bf[:]),
                                     inc="t")
                        ctx.wait("vector", "t", ctx.cnt["t"])
                        ctx.emit("vector", lambda e, grp=grp, half=half, tt=tt:
                                 e.tensor_copy(
                                     slabB[:, grp * 4:(grp + 1) * 4,
                                           tt * 128:(tt + 1) * 128],
                                     ptr2[:, half:half + 512]
                                     .rearrange("p (j c) -> p j c", j=4)), inc="v")
                    tpose2[0] = ctx.cnt["t"]

            if DEV and dev_stop == "proj":
                ctx.wait("gpsimd", "v", ctx.cnt["v"])
                ctx.wait("gpsimd", "s", ctx.cnt["s"])
                ctx.wait("gpsimd", "gm")
                if "dev_xmid" in dev:
                    ctx.dma("gpsimd", dev["dev_xmid"], xmid_dram)
                if "dev_xn2" in dev:
                    ctx.dma("gpsimd", dev["dev_xn2"], slabB_h[:])
                _final_waits(ctx)
                with nc.Block() as block:
                    ctx.finalize(block)
                return nc

            ctx.phase_barrier()

            def finalize_scale(mi, nelem):
                ctx.emit("vector", lambda e, mi=mi: e.reduce_sum(
                    ptmp[:, 0:16], pacc[:, mi * 128:(mi + 1) * 128]
                    .rearrange("p (a b) -> p a b", a=16), axis=AX.X), inc="v")
                ctx.wait("vector", "v")
                ctx.emit("vector", lambda e: e.reduce_sum(
                    st[:, 8:9], ptmp[:, 0:16], axis=AX.X), inc="v")
                ctx.wait("vector", "v")
                ctx.wait("tensor", "v", ctx.cnt["v"])
                ctx.emit("tensor", lambda e: e.transpose(
                    pp_ps2[0:1, 0:128], st[:, 8:9], id_f[:]), inc="t")
                ctx.wait("vector", "t", ctx.cnt["t"])
                ctx.emit("vector", lambda e: e.tensor_copy(
                    prow[:], pp_ps2[0:1, 0:128]), inc="v")
                ctx.wait("vector", "v")
                ctx.emit("vector", lambda e: e.tensor_tensor_scan(
                    srow[:], prow[:], zrow[:], 0.0, ALU.add, ALU.add), inc="v")
                ctx.wait("vector", "v")
                ctx.emit("vector", lambda e: e.tensor_scalar(
                    st[0:1, 16:17], srow[0:1, 127:128],
                    5.9604644775390625e-08, 1e-8, ALU.mult, ALU.add), inc="v")
                ctx.wait("vector", "v")
                ctx.wait("tensor", "v", ctx.cnt["v"])
                ctx.emit("tensor", lambda e: e.matmul(
                    pp_ps2[:, 0:1], onesr_f[:], st[0:1, 16:17],
                    start=True, stop=True), inc="t")
                ctx.wait("vector", "t", ctx.cnt["t"])
                ctx.emit("vector", lambda e, mi=mi: e.tensor_copy(
                    scl[:, 16 * mi:16 * mi + 1], pp_ps2[:, 0:1]), inc="v")
                ctx.wait("vector", "v")
                ctx.emit("vector", lambda e, mi=mi: e.tensor_tensor(
                    scl[:, 16 * mi + 4:16 * mi + 5], scl[:, 16 * mi:16 * mi + 1],
                    scl[:, 48:49], ALU.mult), inc="v")
                ctx.wait("vector", "v")
                ctx.emit("vector", lambda e, mi=mi: e.tensor_scalar(
                    scl[:, 16 * mi + 12:16 * mi + 13],
                    scl[:, 16 * mi:16 * mi + 1], -1.0, None, ALU.mult), inc="v")
                ctx.wait("vector", "v")

            with ExitStack() as p35:
                sb = lambda n, s, d: p35.enter_context(nc.sbuf_tensor(n, s, d))
                wd0 = sb("wd0", [128, 2048], F32)
                wd1 = sb("wd1", [128, 2048], F32)
                tds = sb("tds", [128, 2048], BF16)
                pp_ps2 = p35.enter_context(nc.psum_tensor("pp_ps2", [128, 512], F32))
                tp_ps = p35.enter_context(nc.psum_tensor("tp_ps", [128, 2048], BF16))

                for mi, nel in ((0, DE * D), (1, DE * D), (2, DE * D)):
                    finalize_scale(mi, nel)
                ctx.emit("vector", lambda e: e.tensor_tensor(
                    scl[:, 56:57], scl[:, 32:33], scl[:, 0:1], ALU.mult), inc="v")
                ctx.wait("vector", "v")
                scl_done_v = ctx.cnt["v"]

                qd_par = [0, 0]
                for ci in range(64):
                    r = ci // 4
                    ec = ci % 4
                    par = ci % 2
                    wd = (wd0, wd1)[par]
                    wsm = ("u0", "u1")[par]
                    if ci >= 2:
                        ctx.wait("sync", "v", qd_par[par])
                    ctx.dma("sync", wd[:],
                            w_down[r * 128:(r + 1) * 128, ec * 2048:(ec + 1) * 2048],
                            sem=wsm)
                    ctx.wait("vector", wsm)
                    if ci == 0:
                        ctx.wait("vector", "v", scl_done_v)
                    ctx.emit("vector", lambda e, wd=wd: e.tensor_scalar(
                        w2[:], wd[:], scl[:, 60:61], None, ALU.mult), inc="v")
                    ctx.wait("vector", "v")
                    ctx.emit("vector", lambda e: e.tensor_scalar(
                        w1[:], w2[:], scl[:, 4:5], None, ALU.add), inc="v")
                    ctx.wait("vector", "v")
                    ctx.emit("vector", lambda e, wd=wd: e.tensor_scalar(
                        wd[:], w1[:], scl[:, 12:13], None, ALU.is_lt), inc="v")
                    ctx.wait("vector", "v")
                    ctx.emit("vector", lambda e: e.tensor_scalar(
                        w1[:], w2[:], scl[:, 4:5], None, ALU.subtract), inc="v")
                    ctx.wait("vector", "v")
                    if ci >= 1:
                        ctx.wait("vector", "t", ctx.cnt["t"])
                    ctx.emit("vector", lambda e, wd=wd: e.scalar_tensor_tensor(
                        b1[:], w1[:], scl[:, 0:1], wd[:],
                        ALU.is_gt, ALU.subtract), inc="v")
                    qd_par[par] = ctx.cnt["v"]
                    ctx.wait("tensor", "v", ctx.cnt["v"])
                    if ci >= 1:
                        ctx.wait("tensor", "gt2")
                    for grp in range(4):
                        half = (grp % 2) * 1024
                        if grp >= 2:
                            ctx.wait("tensor", "s", ctx.cnt["s"])
                        for j in range(4):
                            et = grp * 4 + j
                            ctx.emit("tensor", lambda e, et=et, half=half, j=j:
                                     e.transpose(
                                         tp_ps[:, half + j * 128: half + (j + 1) * 128],
                                         b1[:, et * 128:(et + 1) * 128], id_bf[:]),
                                     inc="t")
                        ctx.wait("scalar", "t", ctx.cnt["t"])
                        ctx.emit("scalar", lambda e, grp=grp, half=half:
                                 e.copy(
                                     tds[:, grp * 512:(grp + 1) * 512],
                                     tp_ps[:, half:half + 512]), inc="s")
                    ctx.wait("gpsimd", "s", ctx.cnt["s"])
                    ctx.dma(
                        "gpsimd",
                        tdT_dram[r // 4, :, ec * 16:(ec + 1) * 16,
                                 (r % 4) * 128:(r % 4) * 128 + 128],
                        tds[:].rearrange("p (et c) -> p et c", et=16), sem="gt2")

            if DEV and dev_stop == "wd":
                ctx.wait("gpsimd", "gt2")
                if "dev_tdT" in dev:
                    ctx.dma("gpsimd", dev["dev_tdT"],
                            tdT_dram.rearrange("a b c d -> (a b c d)"))
                if "dev_scales" in dev:
                    ctx.wait("gpsimd", "v", ctx.cnt["v"])
                    ctx.dma("gpsimd", dev["dev_scales"], scl[:, 0:64])
                _final_waits(ctx)
                with nc.Block() as block:
                    ctx.finalize(block)
                return nc

            ctx.phase_barrier()
            with ExitStack() as p4s:
                sb = lambda n, s, d: p4s.enter_context(nc.sbuf_tensor(n, s, d))
                wu0 = sb("wu0", [128, 2048], F32)
                wu1 = sb("wu1", [128, 2048], F32)
                wg0 = sb("wg0", [128, 2048], F32)
                wg1 = sb("wg1", [128, 2048], F32)
                qa = sb("qa", [128, 1024], F32)
                qd = sb("qd", [128, 1024], F32)
                up_ps = p4s.enter_context(nc.psum_tensor("up_ps", [128, 1024], F32))
                gt_ps = p4s.enter_context(nc.psum_tensor("gt_ps", [128, 2048], F32))
                tq_ps = p4s.enter_context(nc.psum_tensor("tq_ps", [128, 2048], BF16))
                ttuA = [slabA_h[:, 0:2048], slabA_h[:, 2048:4096]]
                ttgA = [slabA_h[:, 4096:6144], slabA_h[:, 6144:8192]]
                b1uA = [slabA_h[:, 8192:10240], slabA_h[:, 10240:12288]]
                b1gA = [slabA_h[:, 12288:14336], slabA_h[:, 14336:16384]]
                wsil = [w1, w2]

                def quant_cols(src, b1dst, sname, c0):
                    cs = slice(c0, c0 + 1024)
                    ctx.emit("vector", lambda e, src=src, cs=cs: e.tensor_scalar(
                        qa[:], src[:, cs], scl[:, 60:61], None, ALU.mult),
                        inc="v")
                    ctx.wait("vector", "v")
                    ctx.emit("vector", lambda e, sname=sname: e.tensor_scalar(
                        qd[:], qa[:], scl[:, sname + 4:sname + 5], None,
                        ALU.add), inc="v")
                    ctx.wait("vector", "v")
                    ctx.emit("vector", lambda e, src=src, cs=cs, sname=sname:
                             e.tensor_scalar(
                                 src[:, cs], qd[:],
                                 scl[:, sname + 12:sname + 13], None,
                                 ALU.is_lt), inc="v")
                    ctx.wait("vector", "v")
                    ctx.emit("vector", lambda e, sname=sname: e.tensor_scalar(
                        qd[:], qa[:], scl[:, sname + 4:sname + 5], None,
                        ALU.subtract), inc="v")
                    ctx.wait("vector", "v")
                    ctx.emit("vector", lambda e, src=src, cs=cs, b1dst=b1dst,
                             sname=sname: e.scalar_tensor_tensor(
                                 b1dst[:, cs], qd[:],
                                 scl[:, sname:sname + 1], src[:, cs],
                                 ALU.is_gt, ALU.subtract), inc="v")
                    ctx.wait("vector", "v")

                tph_s = [0, 0]

                def tpose_evac(b1src, dst):
                    for grp in range(4):
                        h = grp % 2
                        half = h * 1024
                        ctx.wait("tensor", "s", tph_s[h])
                        for j in range(4):
                            dt = grp * 4 + j
                            ctx.emit("tensor", lambda e, dt=dt, half=half, j=j,
                                     b1src=b1src: e.transpose(
                                         tq_ps[:, half + j * 128:
                                               half + (j + 1) * 128],
                                         b1src[:, dt * 128:(dt + 1) * 128],
                                         id_bf[:]), inc="t")
                        ctx.wait("scalar", "t", ctx.cnt["t"])
                        ctx.emit("scalar", lambda e, grp=grp, half=half,
                                 dst=dst: e.copy(
                                     dst[:, grp * 512:(grp + 1) * 512],
                                     tq_ps[:, half:half + 512]), inc="s")
                        tph_s[h] = ctx.cnt["s"]

                qv_par = [0, 0]
                tpu_par = [0, 0]
                tpg_par = [0, 0]
                prev = [None]
                prev_silu = [0]

                def emit_hidden(st8):
                    j, silu_s, mmt, wsl = st8
                    ctx.wait("vector", "gh")
                    ctx.wait("vector", "s", silu_s)
                    ctx.wait("vector", "t", mmt)
                    for tc in range(4):
                        ctx.emit("vector", lambda e, tc=tc, wsl=wsl:
                                 e.tensor_tensor(
                                     b2[:, tc * 512:(tc + 1) * 512],
                                     wsl[:, tc * 512:(tc + 1) * 512],
                                     gt_ps[:, tc * 512:(tc + 1) * 512],
                                     ALU.mult), inc="v")
                    ctx.wait("gpsimd", "v", ctx.cnt["v"])
                    ctx.dma("gpsimd", hid_dram[j * 128:(j + 1) * 128, :],
                            b2[:], sem="gh")

                for ec in range(64):
                    par = ec % 2
                    wu = (wu0, wu1)[par]
                    wg = (wg0, wg1)[par]
                    su, sg = ("u0", "u1")[par], ("u2", "u3")[par]
                    if ec >= 2:
                        ctx.wait("sync", "v", qv_par[par])
                    ctx.dma("sync", wu[:], w_up[ec * 128:(ec + 1) * 128, :],
                            sem=su)
                    ctx.dma("sync", wg[:], w_gate[ec * 128:(ec + 1) * 128, :],
                            sem=sg)
                    ctx.wait("vector", su)
                    if ec >= 2:
                        ctx.wait("vector", "t", tpu_par[par])
                    quant_cols(wu, b1uA[par], 16, 0)
                    quant_cols(wu, b1uA[par], 16, 1024)
                    qu_v = ctx.cnt["v"]
                    ctx.wait("vector", sg)
                    if ec >= 2:
                        ctx.wait("vector", "t", tpg_par[par])
                    quant_cols(wg, b1gA[par], 32, 0)
                    quant_cols(wg, b1gA[par], 32, 1024)
                    qv_par[par] = ctx.cnt["v"]
                    qg_v = ctx.cnt["v"]
                    if prev[0] is not None:
                        emit_hidden(prev[0])
                    hid_v = ctx.cnt["v"]
                    ctx.wait("tensor", "v", qu_v)
                    tpose_evac(b1uA[par], ttuA[par])
                    tpu_par[par] = ctx.cnt["t"]
                    ctx.wait("tensor", "v", qg_v)
                    tpose_evac(b1gA[par], ttgA[par])
                    tpg_par[par] = ctx.cnt["t"]
                    ctx.wait("tensor", "s", ctx.cnt["s"])
                    for pair in range(2):
                        if pair == 0 and prev[0] is not None:
                            ctx.wait("tensor", "s", prev_silu[0])
                        if pair == 1:
                            ctx.wait("tensor", "s", ctx.cnt["s"])
                        for dt in range(ND):
                            for sub in range(2):
                                tc = pair * 2 + sub
                                ctx.emit("tensor", lambda e, dt=dt, sub=sub,
                                         tc=tc, par=par: e.matmul(
                                             up_ps[:, sub * 512:(sub + 1) * 512],
                                             ttuA[par][:, dt * 128:(dt + 1) * 128],
                                             slabB[:, dt, tc * 512:(tc + 1) * 512],
                                             start=(dt == 0), stop=(dt == ND - 1)),
                                         inc=("t" if dt == ND - 1 and sub == 1
                                              else None))
                        up_t = ctx.cnt["t"]
                        if pair == 0 and prev[0] is not None:
                            ctx.wait("tensor", "v", hid_v)
                        for dt in range(ND):
                            for sub in range(2):
                                tc = pair * 2 + sub
                                ctx.emit("tensor", lambda e, dt=dt, tc=tc,
                                         par=par: e.matmul(
                                             gt_ps[:, tc * 512:(tc + 1) * 512],
                                             ttgA[par][:, dt * 128:(dt + 1) * 128],
                                             slabB[:, dt, tc * 512:(tc + 1) * 512],
                                             start=(dt == 0), stop=(dt == ND - 1)),
                                         inc=("t" if dt == ND - 1 and sub == 1
                                              else None))
                        ctx.wait("scalar", "t", up_t)
                        if pair == 0 and prev[0] is not None:
                            ctx.wait("scalar", "v", hid_v)
                        for sub in range(2):
                            tc = pair * 2 + sub
                            ctx.emit("scalar", lambda e, sub=sub, tc=tc,
                                     par=par: e.activation(
                                         wsil[par][:, tc * 512:(tc + 1) * 512],
                                         up_ps[:, sub * 512:(sub + 1) * 512],
                                         AF.Silu, scale=scl[:, 16:17]), inc="s")
                    prev_silu[0] = ctx.cnt["s"]
                    prev[0] = (ec, ctx.cnt["s"], ctx.cnt["t"], wsil[par])
                emit_hidden(prev[0])

            if DEV and dev_stop == "ffn_a":
                ctx.wait("gpsimd", "gh")
                if "dev_hid" in dev:
                    ctx.dma("gpsimd", dev["dev_hid"], hid_dram)
                _final_waits(ctx)
                with nc.Block() as block:
                    ctx.finalize(block)
                return nc

            ctx.phase_barrier()
            with ExitStack() as p5s:
                sb = lambda n, s, d: p5s.enter_context(nc.sbuf_tensor(n, s, d))
                td0 = sb("td0", [128, 1024], BF16)
                td1 = sb("td1", [128, 1024], BF16)
                xm0 = sb("xm0", [128, 512], F32)
                xm1 = sb("xm1", [128, 512], F32)
                ost0 = sb("ost0", [128, 512], F32)
                ost1 = sb("ost1", [128, 512], F32)
                po_a = p5s.enter_context(nc.psum_tensor("po_a", [128, 2048], F32))
                po_b = p5s.enter_context(nc.psum_tensor("po_b", [128, 2048], F32))
                slabAf = slabA_h[:]
                slabBf = slabB_h[:]

                def hid_tile(et, tslot):
                    base = (et % 32) * 1024 + tslot * 128
                    src = slabAf if et < 32 else slabBf
                    return src[:, base:base + 128]

                for tc in range(2):
                    if tc == 0:
                        ctx.phase_barrier()
                        for part in range(8):
                            e0 = part * 8
                            dst = slabAf if e0 < 32 else slabBf
                            ctx.dma("sync",
                                    dst[:, (e0 % 32) * 1024:((e0 % 32) + 8) * 1024]
                                    .rearrange("p (et c) -> p et c", et=8),
                                    hid_dram[e0 * 128:(e0 + 8) * 128, 0:1024]
                                    .rearrange("(et p) c -> p et c", p=128),
                                    sem="wh")
                    ctx.wait("tensor", "wh")
                    if tc == 1:
                        ctx.wait("tensor", "gm")
                    for dq in range(4):
                        tb_par = [0, 0]
                        for etp in range(32):
                            par = etp % 2
                            td = (td0, td1)[par]
                            tsm = ("u0", "u1")[par]
                            if etp >= 2:
                                ctx.wait("sync", "t", tb_par[par])
                            ctx.dma("sync",
                                    td[:].rearrange("p (eb c) -> p eb c", eb=2),
                                    tdT_dram[dq, :, etp * 2:etp * 2 + 2, :],
                                    sem=tsm)
                            ctx.wait("tensor", tsm)
                            if etp == 0 and (dq > 0 or tc > 0):
                                ctx.wait("tensor", "v", ctx.cnt["v"])
                            for sub in range(2):
                                et = etp * 2 + sub
                                for tslot in range(8):
                                    bank = po_a if tslot < 4 else po_b
                                    boff = (tslot % 4) * 512
                                    ctx.emit("tensor",
                                             lambda e, et=et, tslot=tslot,
                                             bank=bank, boff=boff, td=td,
                                             sub=sub: e.matmul(
                                                 bank[:, boff:boff + 512],
                                                 hid_tile(et, tslot),
                                                 td[:, sub * 512:(sub + 1) * 512],
                                                 start=(et == 0),
                                                 stop=(et == 63)),
                                             inc=("t" if tslot == 7 else None))
                            tb_par[par] = ctx.cnt["t"]
                            et = etp * 2 + 1
                            if tc == 0 and dq == 3 and et % 8 == 7:
                                e0 = et - 7
                                ctx.wait("gpsimd", "t", ctx.cnt["t"])
                                dst = slabAf if e0 < 32 else slabBf
                                ctx.dma("gpsimd",
                                        dst[:, (e0 % 32) * 1024:
                                            ((e0 % 32) + 8) * 1024]
                                        .rearrange("p (et c) -> p et c", et=8),
                                        hid_dram[e0 * 128:(e0 + 8) * 128,
                                                 1024:2048]
                                        .rearrange("(et p) c -> p et c", p=128),
                                        sem="gm")
                        dn_t = ctx.cnt["t"]
                        for tslot in range(8):
                            par = tslot % 2
                            xm = (xm0, xm1)[par]
                            ost = (ost0, ost1)[par]
                            xsm = ("u2", "u3")[par]
                            r0 = tc * 1024 + tslot * 128
                            if tslot >= 2:
                                ctx.wait("sync", "v", ctx.cnt["v"])
                            ctx.dma("sync", xm[:],
                                    xmid_dram[r0:r0 + 128, dq * 512:(dq + 1) * 512],
                                    sem=xsm)
                            ctx.wait("vector", xsm)
                            ctx.wait("vector", "t", dn_t)
                            osem = ("go", "go2")[par]
                            ctx.wait("vector", osem)
                            bank = po_a if tslot < 4 else po_b
                            boff = (tslot % 4) * 512
                            ctx.emit("vector", lambda e, bank=bank, boff=boff,
                                     xm=xm, ost=ost: e.scalar_tensor_tensor(
                                         ost[:], bank[:, boff:boff + 512],
                                         scl[:, 56:57], xm[:], ALU.mult, ALU.add),
                                     inc="v")
                            ctx.wait("gpsimd", "v", ctx.cnt["v"])
                            ctx.dma("gpsimd", out[r0:r0 + 128,
                                                  dq * 512:(dq + 1) * 512],
                                    ost[:], sem=osem)

        _final_waits(ctx)
        with nc.Block() as block:
            ctx.finalize(block)
    return nc


def _final_waits(ctx):
    for sem, v in ctx.cnt.items():
        if v:
            ctx.wait("gpsimd", sem, v)
            ctx.wait("sync", sem, v)


def _in_map(inputs, c):
    return {
        "xb": np.ascontiguousarray(inputs["L0_attn_x"][c]),
        "Wq": inputs["L0_attn_Wq"], "Wk": inputs["L0_attn_Wk"],
        "Wv": inputs["L0_attn_Wv"], "Wproj": inputs["L0_attn_Wproj"],
        "q_gain": np.ascontiguousarray(inputs["L0_attn_q_gain"].reshape(1, H)),
        "q_A": inputs["L0_attn_q_A"], "q_B": inputs["L0_attn_q_B"],
        "k_A": inputs["L0_attn_k_A"], "k_B": inputs["L0_attn_k_B"],
        "v_A": inputs["L0_attn_v_A"], "v_B": inputs["L0_attn_v_B"],
        "p_A": inputs["L0_attn_p_A"], "p_B": inputs["L0_attn_p_B"],
        "W_up": inputs["L0_ffn_W_up"], "W_gate": inputs["L0_ffn_W_gate"],
        "W_down": inputs["L0_ffn_W_down"],
    }


def kernel(**inputs):
    nc = build_program()
    in_maps = [_in_map(inputs, c) for c in range(B)]
    res = run_bass_kernel_spmd(nc, in_maps, core_ids=list(range(B)))
    return np.stack([res.results[c]["out"] for c in range(B)], axis=0)
